# version 09
# baseline (speedup 1.0000x reference)
import math
import numpy as np
import concourse.bass as bass
import concourse.mybir as mybir
from concourse.bass_utils import run_bass_kernel_spmd
from concourse.alu_op_type import AluOpType
from contextlib import ExitStack

F32 = mybir.dt.float32
BF16 = mybir.dt.bfloat16
AF = mybir.ActivationFunctionType
AX = mybir.AxisListType

H = 4096
FC = 32
HD = 128
TOK = 1024
NG = 8
SCALE = 1.0 / math.sqrt(HD)
SC1 = float(np.float32(SCALE))
SC2 = float(np.float32(SCALE - SC1))
EPS = 1e-5

ENGS = ("sync", "gpsimd", "scalar", "vector", "tensor")


class Sem:
    def __init__(self, ctx, nc, name):
        self.h = ctx.enter_context(nc.semaphore(name))
        self.n = 0


def build(n_groups=NG, do_gate=True, do_out=True, do_ln=True, dbg=False):
    nc = bass.Bass("TRN2", target_bir_lowering=False, debug=False,
                   detect_race_conditions=False)

    x_own = nc.dram_tensor("x_own", [TOK, H], F32, kind="ExternalInput").ap()
    x_par = nc.dram_tensor("x_par", [TOK, H], F32, kind="ExternalInput").ap()
    wqkv = nc.dram_tensor("wqkv", [128, FC, 3 * H], F32, kind="ExternalInput").ap()
    wo = nc.dram_tensor("wo", [128, FC, H], F32, kind="ExternalInput").ap()
    wup = nc.dram_tensor("wup", [128, FC, H], F32, kind="ExternalInput").ap()
    qkvb_h = nc.dram_tensor("qkvb_t", [128, 96], F32, kind="ExternalInput").ap()
    vb_h = nc.dram_tensor("vb_row", [1, H], F32, kind="ExternalInput").ap()
    ob_h = nc.dram_tensor("ob_row", [1, H], F32, kind="ExternalInput").ap()
    gw_h = nc.dram_tensor("gw3", [FC, 128, 4], F32, kind="ExternalInput").ap()
    eb_h = nc.dram_tensor("eb_row", [1, 4], F32, kind="ExternalInput").ap()
    a_h = nc.dram_tensor("a3", [FC, 128, 32], F32, kind="ExternalInput").ap()
    bb_h = nc.dram_tensor("bb", [32, H], F32, kind="ExternalInput").ap()
    id_h = nc.dram_tensor("ident", [128, 128], F32, kind="ExternalInput").ap()
    ones_h = nc.dram_tensor("ones_r", [1, 128], F32, kind="ExternalInput").ap()
    onesc_h = nc.dram_tensor("ones_c", [128, 1], F32, kind="ExternalInput").ap()
    ao_h = nc.dram_tensor("allones", [128, 128], F32, kind="ExternalInput").ap()
    em_h = nc.dram_tensor("emat", [4, 32], F32, kind="ExternalInput").ap()
    out_d = nc.dram_tensor("out", [TOK, H], F32, kind="ExternalOutput").ap()
    if dbg:
        d_xnT = nc.dram_tensor("d_xnT", [128, FC, TOK], F32, kind="ExternalOutput").ap()
        d_kT = nc.dram_tensor("d_kT", [128, 2, 2 * TOK], F32, kind="ExternalOutput").ap()
        d_qT = nc.dram_tensor("d_qT", [128, 2, TOK], F32, kind="ExternalOutput").ap()
        d_Vg = nc.dram_tensor("d_Vg", [128, 16, 512], F32, kind="ExternalOutput").ap()
        d_ctxT = nc.dram_tensor("d_ctxT", [128, FC, TOK], F32, kind="ExternalOutput").ap()
        d_zm = nc.dram_tensor("d_zm", [32, TOK], F32, kind="ExternalOutput").ap()
        d_xnp = nc.dram_tensor("d_xnp", [128, FC, TOK], F32, kind="ExternalOutput").ap()
        d_xbt = nc.dram_tensor("d_xbt", [128, H], F32, kind="ExternalOutput").ap()
        d_st = nc.dram_tensor("d_st", [128, 8], F32, kind="ExternalOutput").ap()
        d_cst = nc.dram_tensor("d_cst", [128, 2], F32, kind="ExternalOutput").ap()
        d_xnt = nc.dram_tensor("d_xnt", [128, H], F32, kind="ExternalOutput").ap()
    xnpt = nc.dram_tensor("xnpt", [128, FC, TOK], BF16).ap()

    plan = {k: [] for k in ENGS}

    def add(eng, fn):
        plan[eng].append(fn)

    def wait(eng, sem, val):
        if val > 0:
            add(eng, lambda e, s=sem.h, v=val: e.wait_ge(s, v))

    def dma(queue, out, in_, sem):
        add(queue, lambda e, o=out, i=in_, s=sem.h: e.dma_start(out=o, in_=i)
            .then_inc(s, 16))
        sem.n += 16
        return sem.n

    def op(eng, fn, sem=None):
        if sem is None:
            add(eng, fn)
            return None
        add(eng, lambda e, f=fn, s=sem.h: f(e).then_inc(s, 1))
        sem.n += 1
        return sem.n

    with ExitStack() as ctx:
        sbt = lambda name, shape, dt: ctx.enter_context(nc.sbuf_tensor(name, shape, dt))
        xnT = sbt("xnT", [128, FC, TOK], BF16)
        ctxT = sbt("ctxT", [128, FC, TOK], BF16)
        kT = sbt("kT", [128, 2, 2 * TOK], BF16)
        Vg = sbt("Vg", [128, 16, 512], BF16)
        qT = sbt("qT", [128, 2, TOK], BF16)
        wring = sbt("wring", [128, 2, 4, 512], BF16)
        xpr = sbt("xpr", [128, 2, TOK], BF16)
        xbt = sbt("xbt", [128, H], BF16)
        xnt = sbt("xnt", [128, H], BF16)
        xrr = sbt("xrr", [128, 2, 512], F32)
        stage = sbt("stage", [128, 1, 8 * 128], BF16)
        outs = sbt("outs", [128, 2, 512], F32)
        acc = sbt("acc", [128, 512], F32)
        bcs = sbt("bcs", [128, 512], F32)
        vbr = sbt("vbr", [1, 512], BF16)
        obr = sbt("obr", [1, 512], BF16)
        bbs = sbt("bbs", [32, 512], BF16)
        asb = sbt("asb", [128, FC, 32], BF16)
        gws = sbt("gws", [128, FC, 4], BF16)
        mkb = sbt("mkb", [128, 4], BF16)
        expt = lambda k: xbt[:, k * 512:(k + 1) * 512]
        mex_a = xbt[0:32, 1024:2048]
        zmT_a = xbt[0:32, 2048:3072]
        mkT_a = xbt[0:4, 3072:4096]
        accb_a = stage[:, 0, 0:512]
        vbbc_a = stage[:, 0, 512:1024]
        lg = sbt("lg", [128, 4], F32)
        stats = sbt("stats", [128, 8], F32)
        qkvb = sbt("qkvb", [128, 96], F32)
        ebr = sbt("ebr", [1, 4], BF16)
        idb = sbt("idb", [128, 128], BF16)
        onesr = sbt("onesr", [1, 128], BF16)
        onesrf = sbt("onesrf", [1, 128], F32)
        onesc = sbt("onesc", [128, 1], BF16)
        aones = sbt("aones", [128, 128], BF16)
        emat = sbt("emat_s", [4, 32], BF16)
        cst = sbt("cst", [128, 2], F32)

        ps = [ctx.enter_context(nc.psum_tensor(f"ps{i}", [128, 512], F32))
              for i in range(8)]

        sT = Sem(ctx, nc, "sT")
        sV = Sem(ctx, nc, "sV")
        sA = Sem(ctx, nc, "sA")
        sW = Sem(ctx, nc, "sW")
        sC = Sem(ctx, nc, "sC")
        sX = Sem(ctx, nc, "sX")
        sSP = Sem(ctx, nc, "sSP")
        sP = Sem(ctx, nc, "sP")
        sO = Sem(ctx, nc, "sO")
        sR = Sem(ctx, nc, "sR")

        dma("gpsimd", qkvb[:, :], qkvb_h[:, :], sC)
        dma("gpsimd", idb[:, :], id_h[:, :], sC)
        idb_rdy = sC.n
        dma("gpsimd", onesr[:, :], ones_h[:, :], sC)
        dma("gpsimd", onesrf[:, :], ones_h[:, :], sC)
        dma("gpsimd", onesc[:, :], onesc_h[:, :], sC)
        dma("gpsimd", aones[:, :], ao_h[:, :], sC)
        dma("gpsimd", emat[:, :], em_h[:, :], sC)
        dma("gpsimd", ebr[:, :], eb_h[:, :], sC)
        for fc in range(FC):
            dma("gpsimd", gws[:, fc, :], gw_h[fc, :, :], sC)
            dma("gpsimd", asb[:, fc, :], a_h[fc, :, :], sC)
        c_all = sC.n

        widx = [0]
        wcons = {}

        def w_push(in_ap, width):
            j = widx[0]
            widx[0] += 1
            slot = j % 2
            if j >= 2:
                wait("gpsimd", sT, wcons[j - 2])
            dma("gpsimd", wring[:, slot, :, 0:width], in_ap, sW)
            return slot, sW.n

        def w_consumed():
            wcons[widx[0] - 1] = sT.n

        pidx = [0]
        pcons = {}

        def p_push(fc0):
            j = pidx[0]
            pidx[0] += 1
            slot = j % 2
            if j >= 2:
                wait("sync", sT, pcons[j - 2])
            if j == 0:
                wait("sync", sSP, 16 * 32)
            dma("sync", xpr[:, slot, :], xnpt[:, fc0, :], sP)
            return slot, sP.n

        def p_consumed():
            pcons[pidx[0] - 1] = sT.n

        op("vector", lambda e: e.memset(cst[:, 0:1], EPS))
        cst_rdy = op("vector", lambda e: e.memset(cst[:, 1:2], SCALE), sV)
        wait("scalar", sV, cst_rdy)

        xls_free_v = 0
        xls_free_a = 0
        xnt_free_t = 0
        spill_done = [0]
        tp_evac = {0: 0, 1: 0}
        xnT_done_v = 0

        for i in range(16 if do_ln else 0):
            src = x_own if i < 8 else x_par
            lr = (i % 8) * 128
            wait("gpsimd", sV, xls_free_v)
            wait("gpsimd", sA, xls_free_a)
            x_rdy = dma("gpsimd", xbt[:, :], src[lr:lr + 128, :], sX)

            wait("vector", sX, x_rdy)
            rs_d = op("vector", lambda e: e.reduce_sum(stats[:, 0:1], xbt[:, :], axis=AX.X), sV)
            wait("vector", sV, rs_d)
            op("vector", lambda e: e.tensor_scalar_mul(stats[:, 2:3], stats[:, 0:1], -1.0 / H))
            m_d = op("vector", lambda e: e.tensor_scalar_mul(stats[:, 4:5], stats[:, 0:1], 1.0 / H), sV)
            wait("vector", sV, m_d)
            op("vector", lambda e: e.tensor_tensor(stats[:, 5:6], stats[:, 4:5],
                                                   stats[:, 4:5], AluOpType.mult))
            wait("scalar", sX, x_rdy)
            wait("scalar", sT, xnt_free_t)
            sq_done = op("scalar", lambda e: e.activation(xnt[:, :], xbt[:, :], AF.Square,
                                                          accum_out=stats[:, 1:2]), sA)
            xls_free_a = sq_done
            wait("vector", sA, sq_done)
            t_d = op("vector", lambda e: e.tensor_scalar_mul(stats[:, 3:4], stats[:, 1:2], 1.0 / H), sV)
            wait("vector", sV, t_d)
            var_done = op("vector", lambda e: e.tensor_sub(stats[:, 3:4], stats[:, 3:4],
                                                           stats[:, 5:6]), sV)
            wait("scalar", sV, var_done)
            std_done = op("scalar", lambda e: e.activation(stats[:, 6:7], stats[:, 3:4],
                                                           AF.Sqrt, bias=cst[:, 0:1]), sA)
            wait("vector", sA, std_done)
            r_d = op("vector", lambda e: e.reciprocal(stats[:, 7:8], stats[:, 6:7]), sV)
            wait("vector", sV, r_d)
            xn_done = op("vector", lambda e: e.tensor_scalar(xnt[:, :], xbt[:, :],
                                                             stats[:, 2:3], stats[:, 7:8],
                                                             AluOpType.add, AluOpType.mult), sV)
            xls_free_v = xn_done

            wait("tensor", sV, xn_done)
            if i == 0:
                wait("tensor", sC, idb_rdy)
            last_t = 0
            for fc in range(FC):
                par = fc % 2
                wait("tensor", sV, tp_evac[par])
                last_t = op("tensor", lambda e, fc=fc, par=par: e.matmul(
                    ps[6 + par][:, 0:128], xnt[:, fc * 128:(fc + 1) * 128], idb[:, :],
                    start=True, stop=True), sT)
                wait("vector", sT, last_t)
                if i < 8:
                    ev = op("vector", lambda e, fc=fc, par=par, lr=lr: e.tensor_copy(
                        xnT[:, fc, lr:lr + 128], ps[6 + par][:, 0:128]), sV)
                else:
                    if fc % 8 == 0:
                        wait("vector", sSP, spill_done[0])
                    ev = op("vector", lambda e, fc=fc, par=par: e.tensor_copy(
                        stage[:, 0, (fc % 8) * 128:(fc % 8 + 1) * 128],
                        ps[6 + par][:, 0:128]), sV)
                    if fc % 8 == 7:
                        wait("sync", sV, ev)
                        spill_done[0] = dma("sync", xnpt[:, fc - 7:fc + 1, lr:lr + 128],
                                            stage[:, 0, :], sSP)
                tp_evac[par] = ev
            xnt_free_t = last_t
            if i == 7:
                xnT_done_v = sV.n

        ln_tail_v = sV.n

        if not do_gate:
            n_groups = 0
        wait("tensor", sC, c_all)
        if not do_ln:
            n_groups = 0
        wait("tensor", sV, ln_tail_v)
        mkt_prev_t = 0
        mkt_evac = 0
        lg_evac = {6: 0, 7: 0}
        for tc in range(8 if do_gate else 0):
            gb = 6 + (tc % 2)
            wait("tensor", sV, lg_evac[gb])
            op("tensor", lambda e, tc=tc, gb=gb: e.matmul(
                ps[gb][:, tc * 4:tc * 4 + 4], onesr[:, :], ebr[:, :],
                start=True, stop=False, skip_group_check=True))
            lg_t = None
            for fc in range(FC):
                lg_t = op("tensor", lambda e, tc=tc, fc=fc, gb=gb: e.matmul(
                    ps[gb][:, tc * 4:tc * 4 + 4], xnT[:, fc, tc * 128:(tc + 1) * 128],
                    gws[:, fc, :], start=False, stop=(fc == FC - 1),
                    skip_group_check=True), sT if fc == FC - 1 else None)
            wait("vector", sT, lg_t)
            if tc > 0:
                wait("vector", sT, mkt_prev_t)
            lg_evac[gb] = op("vector", lambda e, tc=tc, gb=gb: e.tensor_copy(
                lg[:, :], ps[gb][:, tc * 4:tc * 4 + 4]), sV)
            wait("vector", sV, lg_evac[gb])
            rm_d = op("vector", lambda e: e.reduce_max(stats[:, 7:8], lg[:, :], axis=AX.X), sV)
            wait("vector", sV, rm_d)
            e_d = op("vector", lambda e: e.tensor_scalar(mkb[:, 0:1], lg[:, 0:1], stats[:, 7:8],
                                                         None, AluOpType.is_equal), sV)
            wait("vector", sV, e_d)
            op("vector", lambda e: e.tensor_copy(stats[:, 4:5], mkb[:, 0:1]))
            mk_done = None
            for k in (1, 2, 3):
                e_d = op("vector", lambda e, k=k: e.tensor_scalar(stats[:, 5:6], lg[:, k:k + 1],
                                                                  stats[:, 7:8], None,
                                                                  AluOpType.is_equal), sV)
                wait("vector", sV, e_d)
                e_d = op("vector", lambda e: e.tensor_tensor(stats[:, 6:7], stats[:, 5:6],
                                                             stats[:, 4:5], AluOpType.mult), sV)
                wait("vector", sV, e_d)
                mk_done = op("vector", lambda e, k=k: e.tensor_tensor(
                    mkb[:, k:k + 1], stats[:, 5:6], stats[:, 6:7], AluOpType.subtract), sV)
                if k < 3:
                    wait("vector", sV, mk_done)
                    op("vector", lambda e, k=k: e.tensor_tensor(stats[:, 4:5], stats[:, 4:5],
                                                                mkb[:, k:k + 1], AluOpType.add))
            if do_gate is True or do_gate >= 2:
                wait("tensor", sV, mk_done)
                wait("tensor", sV, mkt_evac)
                mkt_prev_t = op("tensor", lambda e, tc=tc: e.matmul(
                    ps[5][0:4, 0:128], mkb[:, 0:4], idb[:, :], start=True, stop=True), sT)
                wait("vector", sT, mkt_prev_t)
                mkt_evac = op("vector", lambda e, tc=tc: e.tensor_copy(
                    mkT_a[:, tc * 128:(tc + 1) * 128], ps[5][0:4, 0:128]), sV)
        mkT_done_v = sV.n

        zm_done_v = 0
        mex_evac = 0
        for t2 in range(2 if (do_gate is True or do_gate >= 3) else 0):
            wait("tensor", sV, mkT_done_v)
            wait("tensor", sV, mex_evac)
            wait("tensor", sV, zm_done_v)
            me_t = op("tensor", lambda e, t2=t2: e.matmul(
                ps[4][0:32, :], emat[:, :], mkT_a[:, t2 * 512:(t2 + 1) * 512],
                start=True, stop=True), sT)
            wait("vector", sT, me_t)
            mex_d = op("vector", lambda e, t2=t2: e.tensor_copy(
                mex_a[:, t2 * 512:(t2 + 1) * 512], ps[4][0:32, :]), sV)
            mex_evac = mex_d
            z_t = None
            for fc in range(FC):
                z_t = op("tensor", lambda e, t2=t2, fc=fc: e.matmul(
                    ps[7][0:32, :], asb[:, fc, :], xnT[:, fc, t2 * 512:(t2 + 1) * 512],
                    start=(fc == 0), stop=(fc == FC - 1)),
                    sT if fc == FC - 1 else None)
            wait("vector", sT, z_t)
            wait("vector", sV, mex_d)
            zm_done_v = op("vector", lambda e, t2=t2: e.tensor_tensor(
                zmT_a[:, t2 * 512:(t2 + 1) * 512], ps[7][0:32, :],
                mex_a[:, t2 * 512:(t2 + 1) * 512], AluOpType.mult), sV)

        gate_tail_v = sV.n

        ev_bank = {b: gate_tail_v for b in range(8)}

        def pass_prologue(n_banks):
            for b in range(n_banks):
                wait("tensor", sV, ev_bank[b])
            wait("tensor", sA, sA.n)

        def emit_pass(n_banks, mm_fn, evac_fn, wslab_aps, wwidth, use_par):
            flush_pending()
            pslot = None
            slot = wready = None
            for fc in range(FC):
                if fc % 4 == 0:
                    slot, wready = w_push(wslab_aps[fc // 4], wwidth)
                    if fc > 0:
                        wcons[widx[0] - 2] = sT.n
                if use_par:
                    if fc > 0:
                        pcons[pidx[0] - 1] = sT.n
                    pslot, pready = p_push(fc)
                if fc == 0:
                    pass_prologue(n_banks)
                if fc % 4 == 0:
                    wait("tensor", sW, wready)
                if use_par:
                    wait("tensor", sP, pready)
                mm_fn(fc, slot, pslot)
            w_consumed()
            if use_par:
                p_consumed()
            outs_ = []
            pass_end_t = sT.n
            for b in range(n_banks):
                wait("vector", sT, pass_end_t)
                outs_.append(evac_fn(b))
            for b in range(n_banks):
                ev_bank[b] = outs_[b]
            return outs_

        u_state = {}
        pending = [None]
        last_exp = [0]
        prev_pv15 = [0]
        prev_pv14 = [0]
        prev_acc14 = [0]
        prev_acc15 = [0]
        sum_read_t = [0]
        accb_sum_t = [0]

        def finalize_sums(u):
            st = u_state[u]
            wait("tensor", sV, st["accb"])
            wait("tensor", sV, ev_bank[4])
            sum_t = op("tensor", lambda e: e.matmul(
                ps[4][:, :], aones[:, :], accb_a, start=True, stop=True), sT)
            accb_sum_t[0] = sum_t
            wait("vector", sT, sum_t)
            r_d = op("vector", lambda e: e.reciprocal(bcs[:, :], ps[4][:, :]), sV)
            ev_bank[4] = r_d
            st["rsum"] = r_d

        def finalize_ctx(u):
            st = u_state[u]
            wait("vector", sV, st["rsum"])
            wait("vector", sT, st["pv"])
            ctx_d = op("vector", lambda e, h=st["h"], qc=st["qc"], pp=st["pp"]:
                       e.tensor_tensor(ctxT[:, h, qc * 512:(qc + 1) * 512],
                                       ps[2 + pp][:, :], bcs[:, :], AluOpType.mult), sV)
            ev_bank[2 + st["pp"]] = ctx_d
            st["ctx"] = ctx_d
            ev_bank[4] = max(ev_bank[4], ctx_d)

        def flush_pending():
            if pending[0] is not None:
                u = pending[0]
                if "rsum" not in u_state[u]:
                    finalize_sums(u)
                if "ctx" not in u_state[u]:
                    finalize_ctx(u)
                pending[0] = None

        u_cnt = [0]
        vbr_cons = [0]
        for g in range(n_groups):
            wait("gpsimd", sT, vbr_cons[0])
            dma("gpsimd", vbr[:, :], vb_h[0:1, g * 512:(g + 1) * 512], sC)
            vb_rdy = sC.n
            flush_pending()
            wait("tensor", sC, vb_rdy)
            wait("tensor", sV, ev_bank[0])
            wait("tensor", sA, sA.n)
            vb_mm = op("tensor", lambda e: e.matmul(
                ps[0][:, :], onesr[:, :], vbr[:, :], start=True, stop=True), sT)
            vbr_cons[0] = vb_mm
            wait("vector", sT, vb_mm)
            vb_cp = op("vector", lambda e: e.tensor_copy(vbbc_a, ps[0][:, :]), sV)
            ev_bank[0] = vb_cp

            for hp in range(2):
                kc0 = hp * 8
                vslabs = [wqkv[:, 4 * j:4 * j + 4, 2 * H + g * 512: 2 * H + (g + 1) * 512]
                          for j in range(FC // 4)]
                use_par = (hp == 1)

                def v_mm(fc, slot, pslot, kc0=kc0, use_par=use_par):
                    for tk in range(8):
                        if use_par:
                            lhs = xpr[:, pslot, tk * 128:(tk + 1) * 128]
                        else:
                            lhs = xnT[:, fc, tk * 128:(tk + 1) * 128]
                        op("tensor", lambda e, lhs=lhs, tk=tk, fc=fc, slot=slot: e.matmul(
                            ps[tk][:, :], lhs, wring[:, slot, fc % 4, 0:512],
                            start=(fc == 0), stop=(fc == FC - 1)),
                           sT if tk == 7 else None)

                def v_evac(b, kc0=kc0, vb_cp=vb_cp):
                    wait("vector", sV, vb_cp)
                    return op("vector", lambda e, b=b, kc0=kc0: e.tensor_tensor(
                        Vg[:, kc0 + b, :], ps[b][:, :], vbbc_a, AluOpType.add), sV)

                emit_pass(8, v_mm, v_evac, vslabs, 512, use_par)

            for sg in range(2):
                ha = 4 * g + 2 * sg
                kslabs = [wqkv[:, 4 * j:4 * j + 4, H + ha * 128: H + (ha + 2) * 128]
                          for j in range(FC // 4)]

                def k_mm(fc, slot, pslot, ha=ha):
                    for hl in range(2):
                        for tc in range(4):
                            if tc < 2:
                                rhs = xnT[:, fc, tc * 512:(tc + 1) * 512]
                            else:
                                rhs = xpr[:, pslot, (tc - 2) * 512:(tc - 1) * 512]
                            op("tensor", lambda e, hl=hl, tc=tc, rhs=rhs, slot=slot, fc=fc:
                               e.matmul(ps[hl * 4 + tc][:, :],
                                        wring[:, slot, fc % 4, hl * 128:(hl + 1) * 128],
                                        rhs, start=(fc == 0), stop=(fc == FC - 1)),
                               sT if (hl == 1 and tc == 3) else None)

                def k_evac(b, ha=ha):
                    hl, tc = b // 4, b % 4
                    return op("vector", lambda e, hl=hl, tc=tc: e.tensor_scalar(
                        kT[:, hl, tc * 512:(tc + 1) * 512], ps[hl * 4 + tc][:, :],
                        qkvb[:, 32 + ha + hl:33 + ha + hl], None, AluOpType.add), sV)

                emit_pass(8, k_mm, k_evac, kslabs, 256, True)

                qslabs = [wqkv[:, 4 * j:4 * j + 4, ha * 128:(ha + 2) * 128]
                          for j in range(FC // 4)]

                def q_mm(fc, slot, pslot, ha=ha):
                    for hl in range(2):
                        for qc in range(2):
                            op("tensor", lambda e, hl=hl, qc=qc, slot=slot, fc=fc:
                               e.matmul(ps[hl * 2 + qc][:, :],
                                        wring[:, slot, fc % 4, hl * 128:(hl + 1) * 128],
                                        xnT[:, fc, qc * 512:(qc + 1) * 512],
                                        start=(fc == 0), stop=(fc == FC - 1)),
                               sT if (hl == 1 and qc == 1) else None)

                def q_evac(b, ha=ha):
                    hl, qc = b // 2, b % 2
                    op("vector", lambda e, hl=hl, qc=qc: e.tensor_scalar(
                        outs[:, 0, :], ps[hl * 2 + qc][:, :],
                        qkvb[:, ha + hl:ha + hl + 1], SC1, AluOpType.add,
                        AluOpType.mult), sV)
                    t2 = op("vector", lambda e, hl=hl, qc=qc: e.tensor_scalar(
                        outs[:, 1, :], ps[hl * 2 + qc][:, :],
                        qkvb[:, ha + hl:ha + hl + 1], SC2, AluOpType.add,
                        AluOpType.mult), sV)
                    wait("vector", sV, t2)
                    return op("vector", lambda e, hl=hl, qc=qc: e.tensor_tensor(
                        qT[:, hl, qc * 512:(qc + 1) * 512], outs[:, 0, :],
                        outs[:, 1, :], AluOpType.add), sV)

                emit_pass(4, q_mm, q_evac, qslabs, 256, False)
                sg_ready = sV.n
                wait("tensor", sV, sg_ready)

                for hl in range(2):
                    h = ha + hl
                    for qc in range(2):
                        u = u_cnt[0]
                        u_cnt[0] += 1
                        pp = u % 2
                        hloc = 2 * sg + hl
                        mm_s = {}
                        exp_done = {}
                        pv_mm = {}
                        acc_add = {}
                        if u >= 2 and (u - 2) in u_state:
                            wait("tensor", sV, u_state[u - 2].get("ctx", 0))
                        for step in range(17):
                            if step < 16:
                                kc = step
                                spar = kc % 2
                                if kc >= 2:
                                    wait("tensor", sA, exp_done[kc - 2])
                                else:
                                    wait("tensor", sA, last_exp[0])
                                mm_s[kc] = op("tensor", lambda e, kc=kc, spar=spar,
                                              hl=hl, qc=qc: e.matmul(
                                                  ps[spar][:, :],
                                                  kT[:, hl, kc * 128:(kc + 1) * 128],
                                                  qT[:, hl, qc * 512:(qc + 1) * 512],
                                                  start=True, stop=True), sT)
                            if step >= 1:
                                k = step - 1
                                wait("scalar", sT, mm_s[k])
                                if k >= 2:
                                    wait("scalar", sT, pv_mm[k - 2])
                                    wait("scalar", sV, acc_add[k - 2])
                                elif k == 0:
                                    wait("scalar", sT, prev_pv14[0])
                                    wait("scalar", sV, prev_acc14[0])
                                else:
                                    wait("scalar", sT, prev_pv15[0])
                                    wait("scalar", sV, prev_acc15[0])
                                exp_done[k] = op("scalar", lambda e, k=k: e.activation(
                                    expt(k % 2), ps[k % 2][:, :], AF.Exp), sA)
                                wait("vector", sA, exp_done[k])
                                if k == 0:
                                    acc_add[0] = op("vector", lambda e: e.tensor_copy(
                                        acc[:, :], expt(0)), sV)
                                else:
                                    wait("vector", sV, acc_add[k - 1])
                                    acc_add[k] = op("vector", lambda e, k=k: e.tensor_tensor(
                                        acc[:, :], acc[:, :], expt(k % 2),
                                        AluOpType.add), sV)
                                wait("tensor", sA, exp_done[k])
                                pv_mm[k] = op("tensor", lambda e, k=k, pp=pp, hloc=hloc:
                                              e.matmul(ps[2 + pp][:, :],
                                                       Vg[:, k, hloc * 128:(hloc + 1) * 128],
                                                       expt(k % 2),
                                                       start=(k == 0), stop=(k == 15)), sT)
                            if step == 2 and pending[0] is not None:
                                finalize_sums(pending[0])
                            if step == 6 and pending[0] is not None:
                                finalize_ctx(pending[0])
                                pending[0] = None
                        last_exp[0] = exp_done[15]
                        prev_pv14[0] = pv_mm[14]
                        prev_pv15[0] = pv_mm[15]
                        prev_acc14[0] = acc_add[14]
                        prev_acc15[0] = acc_add[15]
                        wait("vector", sT, accb_sum_t[0])
                        wait("vector", sV, acc_add[15])
                        accb_d = op("vector", lambda e: e.tensor_copy(
                            accb_a, acc[:, :]), sV)
                        u_state[u] = dict(pv=pv_mm[15], accb=accb_d, h=h, qc=qc, pp=pp)
                        pending[0] = u

        flush_pending()
        attn_tail_v = sV.n
        if u_cnt[0] == 0:
            attn_tail_v = sV.n

        resid_rdy = {}
        resid_free = {0: attn_tail_v, 1: attn_tail_v}
        outs_prev = {0: 0, 1: 0}
        misc_cons = [0]
        for b in range(8):
            ev_bank[b] = max(ev_bank[b], attn_tail_v)
        for oc in range(8 if do_out else 0):
            wait("gpsimd", sT, misc_cons[0])
            dma("gpsimd", obr[:, :], ob_h[0:1, oc * 512:(oc + 1) * 512], sC)
            dma("gpsimd", bbs[:, :], bb_h[:, oc * 512:(oc + 1) * 512], sC)
            misc_rdy = sC.n
            oslabs = ([wo[:, 4 * j:4 * j + 4, oc * 512:(oc + 1) * 512]
                       for j in range(FC // 4)] +
                      [wup[:, 4 * j:4 * j + 4, oc * 512:(oc + 1) * 512]
                       for j in range(FC // 4)])
            last_mm = {}
            slot = wready = None
            for f in range(64):
                if f % 4 == 0:
                    slot, wready = w_push(oslabs[f // 4], 512)
                    if f > 0:
                        wcons[widx[0] - 2] = sT.n
                if f == 0:
                    pass_prologue(8)
                    wait("tensor", sC, misc_rdy)
                    for tc in range(8):
                        op("tensor", lambda e, tc=tc: e.matmul(
                            ps[tc][:, :], onesr[:, :], obr[:, :],
                            start=True, stop=False, skip_group_check=True))
                        op("tensor", lambda e, tc=tc: e.matmul(
                            ps[tc][:, :], zmT_a[:, tc * 128:(tc + 1) * 128], bbs[:, :],
                            start=False, stop=False, skip_group_check=True))
                if f % 4 == 0:
                    wait("tensor", sW, wready)
                src = ctxT if f < 32 else xnT
                fc = f % 32
                for tc in range(8):
                    mm = op("tensor", lambda e, src=src, fc=fc, tc=tc, slot=slot, f=f:
                            e.matmul(ps[tc][:, :], src[:, fc, tc * 128:(tc + 1) * 128],
                                     wring[:, slot, f % 4, 0:512], start=False,
                                     stop=(f == 63), skip_group_check=True),
                            sT if (tc == 7 or f == 63) else None)
                    if f == 63:
                        last_mm[tc] = mm
            w_consumed()
            misc_cons[0] = last_mm[7]
            for tc in range(8):
                oslot = tc % 2
                wait("sync", sV, resid_free[oslot])
                resid_rdy[tc] = dma("sync", xrr[:, oslot, :],
                                    x_own[tc * 128:(tc + 1) * 128,
                                          oc * 512:(oc + 1) * 512], sR)
                wait("vector", sT, last_mm[tc])
                wait("vector", sR, resid_rdy[tc])
                wait("vector", sO, outs_prev[oslot])
                ev = op("vector", lambda e, tc=tc, oslot=oslot: e.tensor_tensor(
                    outs[:, oslot, :], ps[tc][:, :], xrr[:, oslot, :],
                    AluOpType.add), sV)
                ev_bank[tc] = ev
                resid_free[oslot] = ev
                wait("sync", sV, ev)
                outs_prev[oslot] = dma("sync", out_d[tc * 128:(tc + 1) * 128,
                                                     oc * 512:(oc + 1) * 512],
                                       outs[:, oslot, :], sO)
        if dbg:
            sD = Sem(ctx, nc, "sD")
            wait("gpsimd", sV, sV.n)
            wait("gpsimd", sT, sT.n)
            wait("gpsimd", sA, sA.n)
            dma("gpsimd", d_xnT[:, :, :], xnT[:, :, :], sD)
            dma("gpsimd", d_kT[:, :, :], kT[:, :, :], sD)
            dma("gpsimd", d_qT[:, :, :], qT[:, :, :], sD)
            dma("gpsimd", d_Vg[:, :, :], Vg[:, :, :], sD)
            dma("gpsimd", d_ctxT[:, :, :], ctxT[:, :, :], sD)
            dma("gpsimd", d_zm[:, :], zmT_a, sD)
            dma("gpsimd", d_xnp[:, :, :], xnpt[:, :, :], sD)
            dma("gpsimd", d_xbt[:, :], xbt[:, :], sD)
            dma("gpsimd", d_xnt[:, :], xnt[:, :], sD)
            dma("sync", d_st[:, :], stats[:, :], sD) if False else None
            dma("gpsimd", d_st[:, :], stats[:, :], sD)
            dma("gpsimd", d_cst[:, :], cst[:, :], sD)
            wait("gpsimd", sD, sD.n)
        wait("sync", sO, sO.n)

        block = ctx.enter_context(nc.Block())

        def runner(fns):
            def f(eng):
                for fn in fns:
                    fn(eng)
            return f

        block.sync(runner(plan["sync"]))
        block.gpsimd(runner(plan["gpsimd"]))
        block.scalar(runner(plan["scalar"]))
        block.vector(runner(plan["vector"]))
        block.tensor(runner(plan["tensor"]))
    return nc


def kernel(L0_attn_x, L0_attn_attn_qkvw, L0_attn_attn_qkvb, L0_attn_attn_ow,
           L0_attn_attn_ob, L0_ffn_W_up, L0_ffn_gate_w, L0_ffn_expert_bias,
           L0_ffn_lora_A, L0_ffn_lora_B):
    x = np.ascontiguousarray(np.asarray(L0_attn_x, dtype=np.float32))
    wqkv = np.ascontiguousarray(np.asarray(L0_attn_attn_qkvw, dtype=np.float32)
                                .reshape(FC, 128, 3 * H).transpose(1, 0, 2))
    wo = np.ascontiguousarray(np.asarray(L0_attn_attn_ow, dtype=np.float32)
                              .reshape(FC, 128, H).transpose(1, 0, 2))
    wup = np.ascontiguousarray(np.ascontiguousarray(
        np.asarray(L0_ffn_W_up, dtype=np.float32).T)
        .reshape(FC, 128, H).transpose(1, 0, 2))
    qkvb = np.asarray(L0_attn_attn_qkvb, dtype=np.float32)
    qkvb_t = np.ascontiguousarray(qkvb.reshape(96, 128).T)
    vb_row = np.ascontiguousarray(qkvb[2 * H:].reshape(1, H))
    ob_row = np.ascontiguousarray(np.asarray(L0_attn_attn_ob, dtype=np.float32
                                             ).reshape(1, H))
    gw3 = np.ascontiguousarray(np.asarray(L0_ffn_gate_w, dtype=np.float32
                                          ).T.reshape(FC, 128, 4))
    eb_row = np.ascontiguousarray(np.asarray(L0_ffn_expert_bias, dtype=np.float32
                                             ).reshape(1, 4))
    a3 = np.ascontiguousarray(np.asarray(L0_ffn_lora_A, dtype=np.float32
                                         ).transpose(2, 0, 1).reshape(FC, 128, 32))
    bb = np.ascontiguousarray(np.asarray(L0_ffn_lora_B, dtype=np.float32
                                         ).transpose(0, 2, 1).reshape(32, H))
    ident = np.eye(128, dtype=np.float32)
    ones_r = np.ones((1, 128), dtype=np.float32)
    ones_c = np.ones((128, 1), dtype=np.float32)
    allones = np.ones((128, 128), dtype=np.float32)
    emat = np.kron(np.eye(4, dtype=np.float32), np.ones((1, 8), dtype=np.float32))

    nc = build()
    in_maps = []
    for c in range(8):
        b, half = c // 2, c % 2
        xo = np.ascontiguousarray(x[b, half * TOK:(half + 1) * TOK])
        xp = np.ascontiguousarray(x[b, (1 - half) * TOK:(2 - half) * TOK])
        in_maps.append(dict(x_own=xo, x_par=xp, wqkv=wqkv, wo=wo, wup=wup,
                            qkvb_t=qkvb_t, vb_row=vb_row, ob_row=ob_row,
                            gw3=gw3, eb_row=eb_row, a3=a3, bb=bb, ident=ident,
                            ones_r=ones_r, ones_c=ones_c, emat=emat,
                            allones=allones))
    res = run_bass_kernel_spmd(nc, in_maps, core_ids=list(range(8)))
    out = np.empty((4, 2048, H), dtype=np.asarray(res.results[0]["out"]).dtype)
    for c in range(8):
        b, half = c // 2, c % 2
        out[b, half * TOK:(half + 1) * TOK] = res.results[c]["out"]
    return out


# version 12
# speedup vs baseline: 1.1883x; 1.1883x over previous
import math
import numpy as np
import concourse.bass as bass
import concourse.mybir as mybir
from concourse.bass_utils import run_bass_kernel_spmd
from concourse.alu_op_type import AluOpType
from contextlib import ExitStack

F32 = mybir.dt.float32
BF16 = mybir.dt.bfloat16
AF = mybir.ActivationFunctionType
AX = mybir.AxisListType

H = 4096
FC = 32
HD = 128
TOK = 1024
NG = 8
SCALE = 1.0 / math.sqrt(HD)
SC1 = float(np.float32(SCALE))
SC2 = float(np.float32(SCALE - SC1))
EPS = 1e-5

ENGS = ("sync", "gpsimd", "scalar", "vector", "tensor")


class Sem:
    def __init__(self, ctx, nc, name):
        self.h = ctx.enter_context(nc.semaphore(name))
        self.n = 0


def build(n_groups=NG, do_gate=True, do_out=True, do_ln=True, dbg=False):
    nc = bass.Bass("TRN2", target_bir_lowering=False, debug=False,
                   detect_race_conditions=False)

    x_own = nc.dram_tensor("x_own", [TOK, H], F32, kind="ExternalInput").ap()
    x_par = nc.dram_tensor("x_par", [TOK, H], F32, kind="ExternalInput").ap()
    wqkv = nc.dram_tensor("wqkv", [128, FC, 3 * H], F32, kind="ExternalInput").ap()
    wo = nc.dram_tensor("wo", [128, FC, H], F32, kind="ExternalInput").ap()
    wup = nc.dram_tensor("wup", [128, FC, H], F32, kind="ExternalInput").ap()
    qkvb_h = nc.dram_tensor("qkvb_t", [128, 96], F32, kind="ExternalInput").ap()
    vb_h = nc.dram_tensor("vb_row", [1, H], F32, kind="ExternalInput").ap()
    ob_h = nc.dram_tensor("ob_row", [1, H], F32, kind="ExternalInput").ap()
    gw_h = nc.dram_tensor("gw3", [FC, 128, 4], F32, kind="ExternalInput").ap()
    eb_h = nc.dram_tensor("eb_row", [1, 4], F32, kind="ExternalInput").ap()
    a_h = nc.dram_tensor("a3", [FC, 128, 32], F32, kind="ExternalInput").ap()
    bb_h = nc.dram_tensor("bb", [32, H], F32, kind="ExternalInput").ap()
    id_h = nc.dram_tensor("ident", [128, 128], F32, kind="ExternalInput").ap()
    ones_h = nc.dram_tensor("ones_r", [1, 128], F32, kind="ExternalInput").ap()
    onesc_h = nc.dram_tensor("ones_c", [128, 1], F32, kind="ExternalInput").ap()
    ao_h = nc.dram_tensor("allones", [128, 128], F32, kind="ExternalInput").ap()
    em_h = nc.dram_tensor("emat", [4, 32], F32, kind="ExternalInput").ap()
    out_d = nc.dram_tensor("out", [TOK, H], F32, kind="ExternalOutput").ap()
    if dbg:
        d_xnT = nc.dram_tensor("d_xnT", [128, FC, TOK], F32, kind="ExternalOutput").ap()
        d_kT = nc.dram_tensor("d_kT", [128, 2, 2 * TOK], F32, kind="ExternalOutput").ap()
        d_qT = nc.dram_tensor("d_qT", [128, 2, TOK], F32, kind="ExternalOutput").ap()
        d_Vg = nc.dram_tensor("d_Vg", [128, 16, 512], F32, kind="ExternalOutput").ap()
        d_ctxT = nc.dram_tensor("d_ctxT", [128, FC, TOK], F32, kind="ExternalOutput").ap()
        d_zm = nc.dram_tensor("d_zm", [32, TOK], F32, kind="ExternalOutput").ap()
        d_xnp = nc.dram_tensor("d_xnp", [128, FC, TOK], F32, kind="ExternalOutput").ap()
        d_xbt = nc.dram_tensor("d_xbt", [128, H], F32, kind="ExternalOutput").ap()
        d_st = nc.dram_tensor("d_st", [128, 8], F32, kind="ExternalOutput").ap()
        d_cst = nc.dram_tensor("d_cst", [128, 2], F32, kind="ExternalOutput").ap()
        d_xnt = nc.dram_tensor("d_xnt", [128, H], F32, kind="ExternalOutput").ap()
    xnpt = nc.dram_tensor("xnpt", [128, FC, TOK], BF16).ap()

    plan = {k: [] for k in ENGS}

    def add(eng, fn):
        plan[eng].append(fn)

    def wait(eng, sem, val):
        if val > 0:
            add(eng, lambda e, s=sem.h, v=val: e.wait_ge(s, v))

    def dma(queue, out, in_, sem):
        add(queue, lambda e, o=out, i=in_, s=sem.h: e.dma_start(out=o, in_=i)
            .then_inc(s, 16))
        sem.n += 16
        return sem.n

    def op(eng, fn, sem=None):
        if sem is None:
            add(eng, fn)
            return None
        add(eng, lambda e, f=fn, s=sem.h: f(e).then_inc(s, 1))
        sem.n += 1
        return sem.n

    with ExitStack() as ctx:
        sbt = lambda name, shape, dt: ctx.enter_context(nc.sbuf_tensor(name, shape, dt))
        xnT = sbt("xnT", [128, FC, TOK], BF16)
        ctxT = sbt("ctxT", [128, FC, TOK], BF16)
        kT = sbt("kT", [128, 2, 2 * TOK], BF16)
        Vg = sbt("Vg", [128, 16, 512], BF16)
        qT = sbt("qT", [128, 2, TOK], BF16)
        wring = sbt("wring", [128, 3, 2, 512], BF16)
        xpr = sbt("xpr", [128, 3, TOK], BF16)
        xbt = sbt("xbt", [128, H], BF16)
        xnt = sbt("xnt", [128, H], BF16)
        xrr = sbt("xrr", [128, 2, 512], F32)
        stage = sbt("stage", [128, 1, 8 * 128], BF16)
        outs = sbt("outs", [128, 2, 512], F32)
        acc = sbt("acc", [128, 2, 512], F32)
        bcs = sbt("bcs", [128, 512], F32)
        vbr = sbt("vbr", [1, 512], BF16)
        obr = sbt("obr", [1, 2, 512], BF16)
        bbs = sbt("bbs", [32, 2, 512], BF16)
        gws = sbt("gws", [128, FC, 4], BF16)
        mkb = sbt("mkb", [128, 4], BF16)
        expt = lambda k: xbt[:, k * 512:(k + 1) * 512]
        mex_a = xbt[0:32, 0:1024]
        zmT_a = xbt[0:32, 2048:3072]
        mkT_a = xbt[0:4, 3072:4096]
        accb_a = stage[:, 0, 0:512]
        asb_ap = lambda fc: Vg[:, fc // 16, (fc % 16) * 32:(fc % 16) * 32 + 32]
        vbbc_a = stage[:, 0, 512:1024]
        lg = sbt("lg", [128, 4], F32)
        stats = sbt("stats", [128, 8], F32)
        qkvb = sbt("qkvb", [128, 96], F32)
        ebr = sbt("ebr", [1, 4], BF16)
        idb = sbt("idb", [128, 128], BF16)
        onesr = sbt("onesr", [1, 128], BF16)
        aones = sbt("aones", [128, 128], BF16)
        emat = sbt("emat_s", [4, 32], BF16)
        cst = sbt("cst", [128, 2], F32)

        ps = [ctx.enter_context(nc.psum_tensor(f"ps{i}", [128, 512], F32))
              for i in range(8)]

        sT = Sem(ctx, nc, "sT")
        sV = Sem(ctx, nc, "sV")
        sA = Sem(ctx, nc, "sA")
        sW = Sem(ctx, nc, "sW")
        sC = Sem(ctx, nc, "sC")
        sX = Sem(ctx, nc, "sX")
        sSP = Sem(ctx, nc, "sSP")
        sP = Sem(ctx, nc, "sP")
        sO = Sem(ctx, nc, "sO")
        sR = Sem(ctx, nc, "sR")
        sO2 = Sem(ctx, nc, "sO2")

        dma("gpsimd", qkvb[:, :], qkvb_h[:, :], sC)
        dma("gpsimd", idb[:, :], id_h[:, :], sC)
        idb_rdy = sC.n
        dma("gpsimd", onesr[:, :], ones_h[:, :], sC)
        dma("gpsimd", aones[:, :], ao_h[:, :], sC)
        dma("gpsimd", emat[:, :], em_h[:, :], sC)
        dma("gpsimd", ebr[:, :], eb_h[:, :], sC)
        for fc in range(FC):
            dma("gpsimd", gws[:, fc, :], gw_h[fc, :, :], sC)
            dma("gpsimd", asb_ap(fc), a_h[fc, :, :], sC)
        c_all = sC.n

        widx = [0]
        wcons = {}

        def w_push(in_ap, width):
            j = widx[0]
            widx[0] += 1
            slot = j % 3
            if j >= 3:
                wait("gpsimd", sT, wcons[j - 3])
            dma("gpsimd", wring[:, slot, :, 0:width], in_ap, sW)
            return slot, sW.n

        def w_consumed():
            wcons[widx[0] - 1] = sT.n

        pidx = [0]
        pcons = {}

        def p_push(fc0):
            j = pidx[0]
            pidx[0] += 1
            slot = j % 3
            if j >= 3:
                wait("sync", sT, pcons[j - 3])
            if j == 0:
                wait("sync", sSP, 16 * 32)
            dma("sync", xpr[:, slot, :], xnpt[:, fc0, :], sP)
            return slot, sP.n

        def p_consumed():
            pcons[pidx[0] - 1] = sT.n

        op("vector", lambda e: e.memset(cst[:, 0:1], EPS))
        cst_rdy = op("vector", lambda e: e.memset(cst[:, 1:2], SCALE), sV)
        wait("scalar", sV, cst_rdy)

        xls_free_v = 0
        xls_free_a = 0
        xnt_free_t = 0
        spill_done = [0]
        tp_evac = {0: 0, 1: 0}
        xnT_done_v = 0

        for i in range(16 if do_ln else 0):
            src = x_own if i < 8 else x_par
            lr = (i % 8) * 128
            wait("gpsimd", sV, xls_free_v)
            wait("gpsimd", sA, xls_free_a)
            x_rdy = dma("gpsimd", xbt[:, :], src[lr:lr + 128, :], sX)

            wait("vector", sX, x_rdy)
            rs_d = op("vector", lambda e: e.reduce_sum(stats[:, 0:1], xbt[:, :], axis=AX.X), sV)
            wait("vector", sV, rs_d)
            op("vector", lambda e: e.tensor_scalar_mul(stats[:, 2:3], stats[:, 0:1], -1.0 / H))
            m_d = op("vector", lambda e: e.tensor_scalar_mul(stats[:, 4:5], stats[:, 0:1], 1.0 / H), sV)
            wait("vector", sV, m_d)
            op("vector", lambda e: e.tensor_tensor(stats[:, 5:6], stats[:, 4:5],
                                                   stats[:, 4:5], AluOpType.mult))
            wait("scalar", sX, x_rdy)
            wait("scalar", sT, xnt_free_t)
            sq_done = op("scalar", lambda e: e.activation(xnt[:, :], xbt[:, :], AF.Square,
                                                          accum_out=stats[:, 1:2]), sA)
            xls_free_a = sq_done
            wait("vector", sA, sq_done)
            t_d = op("vector", lambda e: e.tensor_scalar_mul(stats[:, 3:4], stats[:, 1:2], 1.0 / H), sV)
            wait("vector", sV, t_d)
            var_done = op("vector", lambda e: e.tensor_sub(stats[:, 3:4], stats[:, 3:4],
                                                           stats[:, 5:6]), sV)
            wait("scalar", sV, var_done)
            std_done = op("scalar", lambda e: e.activation(stats[:, 6:7], stats[:, 3:4],
                                                           AF.Sqrt, bias=cst[:, 0:1]), sA)
            wait("vector", sA, std_done)
            r_d = op("vector", lambda e: e.reciprocal(stats[:, 7:8], stats[:, 6:7]), sV)
            wait("vector", sV, r_d)
            xn_done = op("vector", lambda e: e.tensor_scalar(xnt[:, :], xbt[:, :],
                                                             stats[:, 2:3], stats[:, 7:8],
                                                             AluOpType.add, AluOpType.mult), sV)
            xls_free_v = xn_done

            wait("tensor", sV, xn_done)
            if i == 0:
                wait("tensor", sC, idb_rdy)
            last_t = 0
            for fc in range(FC):
                par = fc % 2
                wait("tensor", sV, tp_evac[par])
                last_t = op("tensor", lambda e, fc=fc, par=par: e.matmul(
                    ps[6 + par][:, 0:128], xnt[:, fc * 128:(fc + 1) * 128], idb[:, :],
                    start=True, stop=True), sT)
                wait("vector", sT, last_t)
                if i < 8:
                    ev = op("vector", lambda e, fc=fc, par=par, lr=lr: e.tensor_copy(
                        xnT[:, fc, lr:lr + 128], ps[6 + par][:, 0:128]), sV)
                else:
                    if fc % 8 == 0:
                        wait("vector", sSP, spill_done[0])
                    ev = op("vector", lambda e, fc=fc, par=par: e.tensor_copy(
                        stage[:, 0, (fc % 8) * 128:(fc % 8 + 1) * 128],
                        ps[6 + par][:, 0:128]), sV)
                    if fc % 8 == 7:
                        wait("sync", sV, ev)
                        spill_done[0] = dma("sync", xnpt[:, fc - 7:fc + 1, lr:lr + 128],
                                            stage[:, 0, :], sSP)
                tp_evac[par] = ev
            xnt_free_t = last_t
            if i == 7:
                xnT_done_v = sV.n

        ln_tail_v = sV.n

        if not do_gate:
            n_groups = 0
        wait("tensor", sC, c_all)
        if not do_ln:
            n_groups = 0
        wait("tensor", sV, ln_tail_v)
        mkt_prev_t = 0
        mkt_evac = 0
        lg_evac = {6: 0, 7: 0}
        for tc in range(8 if do_gate else 0):
            gb = 6 + (tc % 2)
            wait("tensor", sV, lg_evac[gb])
            op("tensor", lambda e, tc=tc, gb=gb: e.matmul(
                ps[gb][:, tc * 4:tc * 4 + 4], onesr[:, :], ebr[:, :],
                start=True, stop=False, skip_group_check=True))
            lg_t = None
            for fc in range(FC):
                lg_t = op("tensor", lambda e, tc=tc, fc=fc, gb=gb: e.matmul(
                    ps[gb][:, tc * 4:tc * 4 + 4], xnT[:, fc, tc * 128:(tc + 1) * 128],
                    gws[:, fc, :], start=False, stop=(fc == FC - 1),
                    skip_group_check=True), sT if fc == FC - 1 else None)
            wait("vector", sT, lg_t)
            if tc > 0:
                wait("vector", sT, mkt_prev_t)
            lg_evac[gb] = op("vector", lambda e, tc=tc, gb=gb: e.tensor_copy(
                lg[:, :], ps[gb][:, tc * 4:tc * 4 + 4]), sV)
            wait("vector", sV, lg_evac[gb])
            rm_d = op("vector", lambda e: e.reduce_max(stats[:, 7:8], lg[:, :], axis=AX.X), sV)
            wait("vector", sV, rm_d)
            e_d = op("vector", lambda e: e.tensor_scalar(mkb[:, 0:1], lg[:, 0:1], stats[:, 7:8],
                                                         None, AluOpType.is_equal), sV)
            wait("vector", sV, e_d)
            op("vector", lambda e: e.tensor_copy(stats[:, 4:5], mkb[:, 0:1]))
            mk_done = None
            for k in (1, 2, 3):
                e_d = op("vector", lambda e, k=k: e.tensor_scalar(stats[:, 5:6], lg[:, k:k + 1],
                                                                  stats[:, 7:8], None,
                                                                  AluOpType.is_equal), sV)
                wait("vector", sV, e_d)
                e_d = op("vector", lambda e: e.tensor_tensor(stats[:, 6:7], stats[:, 5:6],
                                                             stats[:, 4:5], AluOpType.mult), sV)
                wait("vector", sV, e_d)
                mk_done = op("vector", lambda e, k=k: e.tensor_tensor(
                    mkb[:, k:k + 1], stats[:, 5:6], stats[:, 6:7], AluOpType.subtract), sV)
                if k < 3:
                    wait("vector", sV, mk_done)
                    op("vector", lambda e, k=k: e.tensor_tensor(stats[:, 4:5], stats[:, 4:5],
                                                                mkb[:, k:k + 1], AluOpType.add))
            if do_gate is True or do_gate >= 2:
                wait("tensor", sV, mk_done)
                wait("tensor", sV, mkt_evac)
                mkt_prev_t = op("tensor", lambda e, tc=tc: e.matmul(
                    ps[5][0:4, 0:128], mkb[:, 0:4], idb[:, :], start=True, stop=True), sT)
                wait("vector", sT, mkt_prev_t)
                mkt_evac = op("vector", lambda e, tc=tc: e.tensor_copy(
                    mkT_a[:, tc * 128:(tc + 1) * 128], ps[5][0:4, 0:128]), sV)
        mkT_done_v = sV.n

        zm_done_v = 0
        mex_evac = 0
        for t2 in range(2 if (do_gate is True or do_gate >= 3) else 0):
            wait("tensor", sV, mkT_done_v)
            wait("tensor", sV, mex_evac)
            wait("tensor", sV, zm_done_v)
            me_t = op("tensor", lambda e, t2=t2: e.matmul(
                ps[4][0:32, :], emat[:, :], mkT_a[:, t2 * 512:(t2 + 1) * 512],
                start=True, stop=True), sT)
            wait("vector", sT, me_t)
            mex_d = op("vector", lambda e, t2=t2: e.tensor_copy(
                mex_a[:, t2 * 512:(t2 + 1) * 512], ps[4][0:32, :]), sV)
            mex_evac = mex_d
            z_t = None
            for fc in range(FC):
                z_t = op("tensor", lambda e, t2=t2, fc=fc: e.matmul(
                    ps[7][0:32, :], asb_ap(fc), xnT[:, fc, t2 * 512:(t2 + 1) * 512],
                    start=(fc == 0), stop=(fc == FC - 1)),
                    sT if fc == FC - 1 else None)
            wait("vector", sT, z_t)
            wait("vector", sV, mex_d)
            zm_done_v = op("vector", lambda e, t2=t2: e.tensor_tensor(
                zmT_a[:, t2 * 512:(t2 + 1) * 512], ps[7][0:32, :],
                mex_a[:, t2 * 512:(t2 + 1) * 512], AluOpType.mult), sV)

        gate_tail_v = sV.n

        ev_bank = {b: gate_tail_v for b in range(8)}

        def pass_prologue(n_banks):
            for b in range(n_banks):
                wait("tensor", sV, ev_bank[b])
            wait("tensor", sA, sA.n)

        def emit_pass(n_banks, mm_fn, evac_fn, wslab_aps, wwidth, use_par):
            flush_pending()
            pslot = None
            slot = wready = None
            for fc in range(FC):
                if fc % 2 == 0:
                    slot, wready = w_push(wslab_aps[fc // 2], wwidth)
                    if fc > 0:
                        wcons[widx[0] - 2] = sT.n
                if use_par:
                    if fc > 0:
                        pcons[pidx[0] - 1] = sT.n
                    pslot, pready = p_push(fc)
                if fc == 0:
                    pass_prologue(n_banks)
                if fc % 2 == 0:
                    wait("tensor", sW, wready)
                if use_par:
                    wait("tensor", sP, pready)
                mm_fn(fc, slot, pslot)
            w_consumed()
            if use_par:
                p_consumed()
            outs_ = []
            pass_end_t = sT.n
            for b in range(n_banks):
                wait("vector", sT, pass_end_t)
                outs_.append(evac_fn(b))
            for b in range(n_banks):
                ev_bank[b] = outs_[b]
            return outs_

        u_state = {}
        pending = [None]
        last_exp = [0]
        prev_pv15 = [0]
        prev_pv14 = [0]
        prev_acc14 = [0]
        prev_acc15 = [0]
        sum_read_t = [0]
        accb_sum_t = [0]

        def finalize_sums(u):
            st = u_state[u]
            wait("tensor", sV, st["accb"])
            wait("tensor", sV, ev_bank[4])
            sum_t = op("tensor", lambda e: e.matmul(
                ps[4][:, :], aones[:, :], accb_a, start=True, stop=True), sT)
            accb_sum_t[0] = sum_t
            wait("vector", sT, sum_t)
            r_d = op("vector", lambda e: e.reciprocal(bcs[:, :], ps[4][:, :]), sV)
            ev_bank[4] = r_d
            st["rsum"] = r_d

        def finalize_ctx(u):
            st = u_state[u]
            wait("vector", sV, st["rsum"])
            wait("vector", sT, st["pv"])
            ctx_d = op("vector", lambda e, h=st["h"], qc=st["qc"], pp=st["pp"]:
                       e.tensor_tensor(ctxT[:, h, qc * 512:(qc + 1) * 512],
                                       ps[2 + pp][:, :], bcs[:, :], AluOpType.mult), sV)
            ev_bank[2 + st["pp"]] = ctx_d
            st["ctx"] = ctx_d
            ev_bank[4] = max(ev_bank[4], ctx_d)

        def flush_pending():
            if pending[0] is not None:
                u = pending[0]
                if "rsum" not in u_state[u]:
                    finalize_sums(u)
                if "ctx" not in u_state[u]:
                    finalize_ctx(u)
                pending[0] = None

        u_cnt = [0]
        vbr_cons = [0]
        for g in range(n_groups):
            wait("gpsimd", sT, vbr_cons[0])
            dma("gpsimd", vbr[:, :], vb_h[0:1, g * 512:(g + 1) * 512], sC)
            vb_rdy = sC.n
            flush_pending()
            wait("tensor", sC, vb_rdy)
            wait("tensor", sV, ev_bank[0])
            wait("tensor", sA, sA.n)
            vb_mm = op("tensor", lambda e: e.matmul(
                ps[0][:, :], onesr[:, :], vbr[:, :], start=True, stop=True), sT)
            vbr_cons[0] = vb_mm
            wait("vector", sT, vb_mm)
            vb_cp = op("vector", lambda e: e.tensor_copy(vbbc_a, ps[0][:, :]), sV)
            ev_bank[0] = vb_cp

            for hp in range(2):
                kc0 = hp * 8
                vslabs = [wqkv[:, 2 * j:2 * j + 2, 2 * H + g * 512: 2 * H + (g + 1) * 512]
                          for j in range(FC // 2)]
                use_par = (hp == 1)

                def v_mm(fc, slot, pslot, kc0=kc0, use_par=use_par):
                    for tk in range(8):
                        if use_par:
                            lhs = xpr[:, pslot, tk * 128:(tk + 1) * 128]
                        else:
                            lhs = xnT[:, fc, tk * 128:(tk + 1) * 128]
                        op("tensor", lambda e, lhs=lhs, tk=tk, fc=fc, slot=slot: e.matmul(
                            ps[tk][:, :], lhs, wring[:, slot, fc % 2, 0:512],
                            start=(fc == 0), stop=(fc == FC - 1)),
                           sT if tk == 7 else None)

                def v_evac(b, kc0=kc0, vb_cp=vb_cp):
                    wait("vector", sV, vb_cp)
                    return op("vector", lambda e, b=b, kc0=kc0: e.tensor_tensor(
                        Vg[:, kc0 + b, :], ps[b][:, :], vbbc_a, AluOpType.add), sV)

                emit_pass(8, v_mm, v_evac, vslabs, 512, use_par)

            for sg in range(2):
                ha = 4 * g + 2 * sg
                kslabs = [wqkv[:, 2 * j:2 * j + 2, H + ha * 128: H + (ha + 2) * 128]
                          for j in range(FC // 2)]

                def k_mm(fc, slot, pslot, ha=ha):
                    for hl in range(2):
                        for tc in range(4):
                            if tc < 2:
                                rhs = xnT[:, fc, tc * 512:(tc + 1) * 512]
                            else:
                                rhs = xpr[:, pslot, (tc - 2) * 512:(tc - 1) * 512]
                            op("tensor", lambda e, hl=hl, tc=tc, rhs=rhs, slot=slot, fc=fc:
                               e.matmul(ps[hl * 4 + tc][:, :],
                                        wring[:, slot, fc % 2, hl * 128:(hl + 1) * 128],
                                        rhs, start=(fc == 0), stop=(fc == FC - 1)),
                               sT if (hl == 1 and tc == 3) else None)

                def k_evac(b, ha=ha):
                    hl, tc = b // 4, b % 4
                    return op("vector", lambda e, hl=hl, tc=tc: e.tensor_scalar(
                        kT[:, hl, tc * 512:(tc + 1) * 512], ps[hl * 4 + tc][:, :],
                        qkvb[:, 32 + ha + hl:33 + ha + hl], None, AluOpType.add), sV)

                emit_pass(8, k_mm, k_evac, kslabs, 256, True)

                qslabs = [wqkv[:, 2 * j:2 * j + 2, ha * 128:(ha + 2) * 128]
                          for j in range(FC // 2)]

                def q_mm(fc, slot, pslot, ha=ha):
                    for hl in range(2):
                        for qc in range(2):
                            op("tensor", lambda e, hl=hl, qc=qc, slot=slot, fc=fc:
                               e.matmul(ps[hl * 2 + qc][:, :],
                                        wring[:, slot, fc % 2, hl * 128:(hl + 1) * 128],
                                        xnT[:, fc, qc * 512:(qc + 1) * 512],
                                        start=(fc == 0), stop=(fc == FC - 1)),
                               sT if (hl == 1 and qc == 1) else None)

                def q_evac(b, ha=ha):
                    hl, qc = b // 2, b % 2
                    op("vector", lambda e, hl=hl, qc=qc: e.tensor_scalar(
                        outs[:, 0, :], ps[hl * 2 + qc][:, :],
                        qkvb[:, ha + hl:ha + hl + 1], SC1, AluOpType.add,
                        AluOpType.mult), sV)
                    t2 = op("vector", lambda e, hl=hl, qc=qc: e.tensor_scalar(
                        outs[:, 1, :], ps[hl * 2 + qc][:, :],
                        qkvb[:, ha + hl:ha + hl + 1], SC2, AluOpType.add,
                        AluOpType.mult), sV)
                    wait("vector", sV, t2)
                    return op("vector", lambda e, hl=hl, qc=qc: e.tensor_tensor(
                        qT[:, hl, qc * 512:(qc + 1) * 512], outs[:, 0, :],
                        outs[:, 1, :], AluOpType.add), sV)

                emit_pass(4, q_mm, q_evac, qslabs, 256, False)
                sg_ready = sV.n
                wait("tensor", sV, sg_ready)

                for hl in range(2):
                    h = ha + hl
                    for qc in range(2):
                        u = u_cnt[0]
                        u_cnt[0] += 1
                        pp = u % 2
                        hloc = 2 * sg + hl
                        SB = (0, 1, 5)
                        mm_s = {}
                        exp_done = {}
                        pv_mm = {}
                        acc_add = {}
                        if u >= 2 and (u - 2) in u_state:
                            wait("tensor", sV, u_state[u - 2].get("ctx", 0))
                        for step in range(18):
                            if step < 16:
                                kc = step
                                spar = SB[kc % 3]
                                if kc >= 3:
                                    wait("tensor", sA, exp_done[kc - 3])
                                elif kc == 0:
                                    wait("tensor", sA, last_exp[0])
                                mm_s[kc] = op("tensor", lambda e, kc=kc, spar=spar,
                                              hl=hl, qc=qc: e.matmul(
                                                  ps[spar][:, :],
                                                  kT[:, hl, kc * 128:(kc + 1) * 128],
                                                  qT[:, hl, qc * 512:(qc + 1) * 512],
                                                  start=True, stop=True), sT)
                            if 1 <= step <= 16:
                                k = step - 1
                                wait("scalar", sT, mm_s[k])
                                if k >= 4:
                                    wait("scalar", sT, pv_mm[k - 4])
                                    wait("scalar", sV, acc_add[k - 4])
                                elif k < 2:
                                    wait("scalar", sT, prev_pv14[0])
                                    wait("scalar", sV, prev_acc14[0])
                                else:
                                    wait("scalar", sT, prev_pv15[0])
                                    wait("scalar", sV, prev_acc15[0])
                                exp_done[k] = op("scalar", lambda e, k=k, sb=SB[k % 3]:
                                                 e.activation(expt(k % 4), ps[sb][:, :],
                                                              AF.Exp), sA)
                                wait("vector", sA, exp_done[k])
                                if k < 2:
                                    acc_add[k] = op("vector", lambda e, k=k: e.tensor_copy(
                                        acc[:, k % 2, :], expt(k % 4)), sV)
                                else:
                                    wait("vector", sV, acc_add[k - 2])
                                    acc_add[k] = op("vector", lambda e, k=k: e.tensor_tensor(
                                        acc[:, k % 2, :], acc[:, k % 2, :], expt(k % 4),
                                        AluOpType.add), sV)
                            if 2 <= step <= 17:
                                k = step - 2
                                wait("tensor", sA, exp_done[k])
                                pv_mm[k] = op("tensor", lambda e, k=k, pp=pp, hloc=hloc:
                                              e.matmul(ps[2 + pp][:, :],
                                                       Vg[:, k, hloc * 128:(hloc + 1) * 128],
                                                       expt(k % 4),
                                                       start=(k == 0), stop=(k == 15)), sT)
                            if step == 2 and pending[0] is not None:
                                finalize_sums(pending[0])
                            if step == 6 and pending[0] is not None:
                                finalize_ctx(pending[0])
                                pending[0] = None
                        last_exp[0] = exp_done[15]
                        prev_pv14[0] = pv_mm[13]
                        prev_pv15[0] = pv_mm[15]
                        prev_acc14[0] = acc_add[13]
                        prev_acc15[0] = acc_add[15]
                        wait("vector", sT, accb_sum_t[0])
                        wait("vector", sV, acc_add[15])
                        accb_d = op("vector", lambda e: e.tensor_tensor(
                            accb_a, acc[:, 0, :], acc[:, 1, :], AluOpType.add), sV)
                        u_state[u] = dict(pv=pv_mm[15], accb=accb_d, h=h, qc=qc, pp=pp)
                        pending[0] = u

        flush_pending()
        attn_tail_v = sV.n
        if u_cnt[0] == 0:
            attn_tail_v = sV.n

        resid_rdy = {}
        resid_free = {0: attn_tail_v, 1: attn_tail_v}
        outs_prev = {0: 0, 1: 0}
        misc_cons = {0: 0, 1: 0}
        misc_rdy_at = {}
        for b in range(8):
            ev_bank[b] = max(ev_bank[b], attn_tail_v)
        if do_out:
            dma("gpsimd", obr[:, 0, :], ob_h[0:1, 0:512], sC)
            dma("gpsimd", bbs[:, 0, :], bb_h[:, 0:512], sC)
            misc_rdy_at[0] = sC.n
        for oc in range(8 if do_out else 0):
            if oc + 1 < 8:
                nsl = (oc + 1) % 2
                wait("gpsimd", sT, misc_cons[nsl])
                dma("gpsimd", obr[:, nsl, :],
                    ob_h[0:1, (oc + 1) * 512:(oc + 2) * 512], sC)
                dma("gpsimd", bbs[:, nsl, :],
                    bb_h[:, (oc + 1) * 512:(oc + 2) * 512], sC)
                misc_rdy_at[oc + 1] = sC.n
            misc_rdy = misc_rdy_at[oc]
            oslabs = ([wo[:, 2 * j:2 * j + 2, oc * 512:(oc + 1) * 512]
                       for j in range(FC // 2)] +
                      [wup[:, 2 * j:2 * j + 2, oc * 512:(oc + 1) * 512]
                       for j in range(FC // 2)])
            last_mm = {}
            slot = wready = None
            for f in range(64):
                if f % 2 == 0:
                    slot, wready = w_push(oslabs[f // 2], 512)
                    if f > 0:
                        wcons[widx[0] - 2] = sT.n
                if f == 0:
                    pass_prologue(8)
                    wait("tensor", sC, misc_rdy)
                    for tc in range(8):
                        op("tensor", lambda e, tc=tc, oc=oc: e.matmul(
                            ps[tc][:, :], onesr[:, :], obr[:, oc % 2, :],
                            start=True, stop=False, skip_group_check=True))
                        op("tensor", lambda e, tc=tc, oc=oc: e.matmul(
                            ps[tc][:, :], zmT_a[:, tc * 128:(tc + 1) * 128],
                            bbs[:, oc % 2, :],
                            start=False, stop=False, skip_group_check=True))
                if f % 2 == 0:
                    wait("tensor", sW, wready)
                src = ctxT if f < 32 else xnT
                fc = f % 32
                for tc in range(8):
                    mm = op("tensor", lambda e, src=src, fc=fc, tc=tc, slot=slot, f=f:
                            e.matmul(ps[tc][:, :], src[:, fc, tc * 128:(tc + 1) * 128],
                                     wring[:, slot, f % 2, 0:512], start=False,
                                     stop=(f == 63), skip_group_check=True),
                            sT if (tc == 7 or f == 63) else None)
                    if f == 63:
                        last_mm[tc] = mm
            w_consumed()
            misc_cons[oc % 2] = last_mm[7]
            for tc in range(8):
                rsl = tc % 2
                osl = tc % 2
                wait("sync", sV, resid_free[rsl])
                resid_rdy[tc] = dma("sync", xrr[:, rsl, :],
                                    x_own[tc * 128:(tc + 1) * 128,
                                          oc * 512:(oc + 1) * 512], sR)
                wait("vector", sT, last_mm[tc])
                wait("vector", sR, resid_rdy[tc])
                wait("vector", sO, outs_prev[osl])
                ev = op("vector", lambda e, tc=tc, osl=osl, rsl=rsl: e.tensor_tensor(
                    outs[:, osl, :], ps[tc][:, :], xrr[:, rsl, :],
                    AluOpType.add), sV)
                ev_bank[tc] = ev
                resid_free[rsl] = ev
                wait("sync", sV, ev)
                outs_prev[osl] = dma("sync", out_d[tc * 128:(tc + 1) * 128,
                                                   oc * 512:(oc + 1) * 512],
                                     outs[:, osl, :], sO)
        if dbg:
            sD = Sem(ctx, nc, "sD")
            wait("gpsimd", sV, sV.n)
            wait("gpsimd", sT, sT.n)
            wait("gpsimd", sA, sA.n)
            dma("gpsimd", d_xnT[:, :, :], xnT[:, :, :], sD)
            dma("gpsimd", d_kT[:, :, :], kT[:, :, :], sD)
            dma("gpsimd", d_qT[:, :, :], qT[:, :, :], sD)
            dma("gpsimd", d_Vg[:, :, :], Vg[:, :, :], sD)
            dma("gpsimd", d_ctxT[:, :, :], ctxT[:, :, :], sD)
            dma("gpsimd", d_zm[:, :], zmT_a, sD)
            dma("gpsimd", d_xnp[:, :, :], xnpt[:, :, :], sD)
            dma("gpsimd", d_xbt[:, :], xbt[:, :], sD)
            dma("gpsimd", d_xnt[:, :], xnt[:, :], sD)
            dma("sync", d_st[:, :], stats[:, :], sD) if False else None
            dma("gpsimd", d_st[:, :], stats[:, :], sD)
            dma("gpsimd", d_cst[:, :], cst[:, :], sD)
            wait("gpsimd", sD, sD.n)
        wait("sync", sO, sO.n)

        block = ctx.enter_context(nc.Block())

        def runner(fns):
            def f(eng):
                for fn in fns:
                    fn(eng)
            return f

        block.sync(runner(plan["sync"]))
        block.gpsimd(runner(plan["gpsimd"]))
        block.scalar(runner(plan["scalar"]))
        block.vector(runner(plan["vector"]))
        block.tensor(runner(plan["tensor"]))
    return nc


def kernel(L0_attn_x, L0_attn_attn_qkvw, L0_attn_attn_qkvb, L0_attn_attn_ow,
           L0_attn_attn_ob, L0_ffn_W_up, L0_ffn_gate_w, L0_ffn_expert_bias,
           L0_ffn_lora_A, L0_ffn_lora_B):
    x = np.ascontiguousarray(np.asarray(L0_attn_x, dtype=np.float32))
    wqkv = np.ascontiguousarray(np.asarray(L0_attn_attn_qkvw, dtype=np.float32)
                                .reshape(FC, 128, 3 * H).transpose(1, 0, 2))
    wo = np.ascontiguousarray(np.asarray(L0_attn_attn_ow, dtype=np.float32)
                              .reshape(FC, 128, H).transpose(1, 0, 2))
    wup = np.ascontiguousarray(np.ascontiguousarray(
        np.asarray(L0_ffn_W_up, dtype=np.float32).T)
        .reshape(FC, 128, H).transpose(1, 0, 2))
    qkvb = np.asarray(L0_attn_attn_qkvb, dtype=np.float32)
    qkvb_t = np.ascontiguousarray(qkvb.reshape(96, 128).T)
    vb_row = np.ascontiguousarray(qkvb[2 * H:].reshape(1, H))
    ob_row = np.ascontiguousarray(np.asarray(L0_attn_attn_ob, dtype=np.float32
                                             ).reshape(1, H))
    gw3 = np.ascontiguousarray(np.asarray(L0_ffn_gate_w, dtype=np.float32
                                          ).T.reshape(FC, 128, 4))
    eb_row = np.ascontiguousarray(np.asarray(L0_ffn_expert_bias, dtype=np.float32
                                             ).reshape(1, 4))
    a3 = np.ascontiguousarray(np.asarray(L0_ffn_lora_A, dtype=np.float32
                                         ).transpose(2, 0, 1).reshape(FC, 128, 32))
    bb = np.ascontiguousarray(np.asarray(L0_ffn_lora_B, dtype=np.float32
                                         ).transpose(0, 2, 1).reshape(32, H))
    ident = np.eye(128, dtype=np.float32)
    ones_r = np.ones((1, 128), dtype=np.float32)
    ones_c = np.ones((128, 1), dtype=np.float32)
    allones = np.ones((128, 128), dtype=np.float32)
    emat = np.kron(np.eye(4, dtype=np.float32), np.ones((1, 8), dtype=np.float32))

    nc = build()
    in_maps = []
    for c in range(8):
        b, half = c // 2, c % 2
        xo = np.ascontiguousarray(x[b, half * TOK:(half + 1) * TOK])
        xp = np.ascontiguousarray(x[b, (1 - half) * TOK:(2 - half) * TOK])
        in_maps.append(dict(x_own=xo, x_par=xp, wqkv=wqkv, wo=wo, wup=wup,
                            qkvb_t=qkvb_t, vb_row=vb_row, ob_row=ob_row,
                            gw3=gw3, eb_row=eb_row, a3=a3, bb=bb, ident=ident,
                            ones_r=ones_r, ones_c=ones_c, emat=emat,
                            allones=allones))
    res = run_bass_kernel_spmd(nc, in_maps, core_ids=list(range(8)))
    out = np.empty((4, 2048, H), dtype=np.asarray(res.results[0]["out"]).dtype)
    for c in range(8):
        b, half = c // 2, c % 2
        out[b, half * TOK:(half + 1) * TOK] = res.results[c]["out"]
    return out


# version 13
# speedup vs baseline: 1.2354x; 1.0397x over previous
import math
import numpy as np
import concourse.bass as bass
import concourse.mybir as mybir
from concourse.bass_utils import run_bass_kernel_spmd
from concourse.alu_op_type import AluOpType
from contextlib import ExitStack

F32 = mybir.dt.float32
BF16 = mybir.dt.bfloat16
AF = mybir.ActivationFunctionType
AX = mybir.AxisListType

H = 4096
FC = 32
HD = 128
TOK = 1024
NG = 8
SCALE = 1.0 / math.sqrt(HD)
SC1 = float(np.float32(SCALE))
SC2 = float(np.float32(SCALE - SC1))
EPS = 1e-5

ENGS = ("sync", "gpsimd", "scalar", "vector", "tensor")


class Sem:
    def __init__(self, ctx, nc, name):
        self.h = ctx.enter_context(nc.semaphore(name))
        self.n = 0


def build(n_groups=NG, do_gate=True, do_out=True, do_ln=True, dbg=False):
    nc = bass.Bass("TRN2", target_bir_lowering=False, debug=False,
                   detect_race_conditions=False)

    x_own = nc.dram_tensor("x_own", [TOK, H], F32, kind="ExternalInput").ap()
    x_par = nc.dram_tensor("x_par", [TOK, H], F32, kind="ExternalInput").ap()
    wqkv = nc.dram_tensor("wqkv", [128, FC, 3 * H], F32, kind="ExternalInput").ap()
    wo = nc.dram_tensor("wo", [128, FC, H], F32, kind="ExternalInput").ap()
    wup = nc.dram_tensor("wup", [128, FC, H], F32, kind="ExternalInput").ap()
    qkvb_h = nc.dram_tensor("qkvb_t", [128, 96], F32, kind="ExternalInput").ap()
    vb_h = nc.dram_tensor("vb_row", [1, H], F32, kind="ExternalInput").ap()
    ob_h = nc.dram_tensor("ob_row", [1, H], F32, kind="ExternalInput").ap()
    gw_h = nc.dram_tensor("gw3", [FC, 128, 4], F32, kind="ExternalInput").ap()
    eb_h = nc.dram_tensor("eb_row", [1, 4], F32, kind="ExternalInput").ap()
    a_h = nc.dram_tensor("a3", [FC, 128, 32], F32, kind="ExternalInput").ap()
    bb_h = nc.dram_tensor("bb", [32, H], F32, kind="ExternalInput").ap()
    id_h = nc.dram_tensor("ident", [128, 128], F32, kind="ExternalInput").ap()
    ones_h = nc.dram_tensor("ones_r", [1, 128], F32, kind="ExternalInput").ap()
    onesc_h = nc.dram_tensor("ones_c", [128, 1], F32, kind="ExternalInput").ap()
    ao_h = nc.dram_tensor("allones", [128, 128], F32, kind="ExternalInput").ap()
    em_h = nc.dram_tensor("emat", [4, 32], F32, kind="ExternalInput").ap()
    out_d = nc.dram_tensor("out", [TOK, H], F32, kind="ExternalOutput").ap()
    if dbg:
        d_xnT = nc.dram_tensor("d_xnT", [128, FC, TOK], F32, kind="ExternalOutput").ap()
        d_kT = nc.dram_tensor("d_kT", [128, 2, 2 * TOK], F32, kind="ExternalOutput").ap()
        d_qT = nc.dram_tensor("d_qT", [128, 2, TOK], F32, kind="ExternalOutput").ap()
        d_Vg = nc.dram_tensor("d_Vg", [128, 16, 512], F32, kind="ExternalOutput").ap()
        d_ctxT = nc.dram_tensor("d_ctxT", [128, FC, TOK], F32, kind="ExternalOutput").ap()
        d_zm = nc.dram_tensor("d_zm", [32, TOK], F32, kind="ExternalOutput").ap()
        d_xnp = nc.dram_tensor("d_xnp", [128, FC, TOK], F32, kind="ExternalOutput").ap()
        d_xbt = nc.dram_tensor("d_xbt", [128, H], F32, kind="ExternalOutput").ap()
        d_st = nc.dram_tensor("d_st", [128, 8], F32, kind="ExternalOutput").ap()
        d_cst = nc.dram_tensor("d_cst", [128, 2], F32, kind="ExternalOutput").ap()
        d_xnt = nc.dram_tensor("d_xnt", [128, H], F32, kind="ExternalOutput").ap()
    xnpt = nc.dram_tensor("xnpt", [128, FC, TOK], BF16).ap()

    plan = {k: [] for k in ENGS}

    def add(eng, fn):
        plan[eng].append(fn)

    def wait(eng, sem, val):
        if val > 0:
            add(eng, lambda e, s=sem.h, v=val: e.wait_ge(s, v))

    def dma(queue, out, in_, sem):
        add(queue, lambda e, o=out, i=in_, s=sem.h: e.dma_start(out=o, in_=i)
            .then_inc(s, 16))
        sem.n += 16
        return sem.n

    def op(eng, fn, sem=None):
        if sem is None:
            add(eng, fn)
            return None
        add(eng, lambda e, f=fn, s=sem.h: f(e).then_inc(s, 1))
        sem.n += 1
        return sem.n

    with ExitStack() as ctx:
        sbt = lambda name, shape, dt: ctx.enter_context(nc.sbuf_tensor(name, shape, dt))
        xnT = sbt("xnT", [128, FC, TOK], BF16)
        ctxT = sbt("ctxT", [128, FC, TOK], BF16)
        kT = sbt("kT", [128, 2, 2 * TOK], BF16)
        Vg = sbt("Vg", [128, 16, 512], BF16)
        qT = sbt("qT", [128, 2, TOK], BF16)
        wring = sbt("wring", [128, 3, 2, 512], BF16)
        xpr = sbt("xpr", [128, 3, TOK], BF16)
        xbt = sbt("xbt", [128, H], BF16)
        xnt = sbt("xnt", [128, H], BF16)
        xrr = sbt("xrr", [128, 2, 512], F32)
        stage = sbt("stage", [128, 1, 8 * 128], BF16)
        outs = sbt("outs", [128, 2, 512], F32)
        acc = sbt("acc", [128, 2, 512], F32)
        bcs = sbt("bcs", [128, 512], F32)
        vbr = sbt("vbr", [1, 512], BF16)
        obr = sbt("obr", [1, 2, 512], BF16)
        bbs = sbt("bbs", [32, 2, 512], BF16)
        gws = sbt("gws", [128, FC, 4], BF16)
        mkb = sbt("mkb", [128, 4], BF16)
        expt = lambda k: xbt[:, k * 512:(k + 1) * 512]
        mex_a = xbt[0:32, 0:1024]
        zmT_a = xbt[0:32, 2048:3072]
        mkT_a = xbt[0:4, 3072:4096]
        accb_a = stage[:, 0, 0:512]
        asb_ap = lambda fc: Vg[:, fc // 16, (fc % 16) * 32:(fc % 16) * 32 + 32]
        vbbc_a = stage[:, 0, 512:1024]
        lg = sbt("lg", [128, 4], F32)
        stats = sbt("stats", [128, 8], F32)
        qkvb = sbt("qkvb", [128, 96], F32)
        ebr = sbt("ebr", [1, 4], BF16)
        idb = sbt("idb", [128, 128], BF16)
        onesr = sbt("onesr", [1, 128], BF16)
        aones = sbt("aones", [128, 128], BF16)
        emat = sbt("emat_s", [4, 32], BF16)
        cst = sbt("cst", [128, 2], F32)

        ps = [ctx.enter_context(nc.psum_tensor(f"ps{i}", [128, 512], F32))
              for i in range(8)]

        sT = Sem(ctx, nc, "sT")
        sV = Sem(ctx, nc, "sV")
        sA = Sem(ctx, nc, "sA")
        sW = Sem(ctx, nc, "sW")
        sC = Sem(ctx, nc, "sC")
        sX = Sem(ctx, nc, "sX")
        sSP = Sem(ctx, nc, "sSP")
        sP = Sem(ctx, nc, "sP")
        sO = Sem(ctx, nc, "sO")
        sR = Sem(ctx, nc, "sR")
        sO2 = Sem(ctx, nc, "sO2")

        dma("gpsimd", qkvb[:, :], qkvb_h[:, :], sC)
        dma("gpsimd", idb[:, :], id_h[:, :], sC)
        idb_rdy = sC.n
        dma("gpsimd", onesr[:, :], ones_h[:, :], sC)
        dma("gpsimd", aones[:, :], ao_h[:, :], sC)
        dma("gpsimd", emat[:, :], em_h[:, :], sC)
        dma("gpsimd", ebr[:, :], eb_h[:, :], sC)
        for fc in range(FC):
            dma("gpsimd", gws[:, fc, :], gw_h[fc, :, :], sC)
            dma("gpsimd", asb_ap(fc), a_h[fc, :, :], sC)
        c_all = sC.n

        widx = [0]
        wcons = {}

        def w_push(in_ap, width):
            j = widx[0]
            widx[0] += 1
            slot = j % 3
            if j >= 3:
                wait("gpsimd", sT, wcons[j - 3])
            dma("gpsimd", wring[:, slot, :, 0:width], in_ap, sW)
            return slot, sW.n

        def w_consumed():
            wcons[widx[0] - 1] = sT.n

        pidx = [0]
        pcons = {}

        def p_push(fc0):
            j = pidx[0]
            pidx[0] += 1
            slot = j % 3
            if j >= 3:
                wait("sync", sT, pcons[j - 3])
            if j == 0:
                wait("sync", sSP, 16 * 32)
            dma("sync", xpr[:, slot, :], xnpt[:, fc0, :], sP)
            return slot, sP.n

        def p_consumed():
            pcons[pidx[0] - 1] = sT.n

        op("vector", lambda e: e.memset(cst[:, 0:1], EPS))
        cst_rdy = op("vector", lambda e: e.memset(cst[:, 1:2], SCALE), sV)
        wait("scalar", sV, cst_rdy)

        xls_free_v = 0
        xls_free_a = 0
        xnt_free_t = 0
        spill_done = [0]
        tp_evac = {0: 0, 1: 0}
        xnT_done_v = 0

        for i in range(16 if do_ln else 0):
            src = x_own if i < 8 else x_par
            lr = (i % 8) * 128
            wait("gpsimd", sV, xls_free_v)
            wait("gpsimd", sA, xls_free_a)
            x_rdy = dma("gpsimd", xbt[:, :], src[lr:lr + 128, :], sX)

            wait("vector", sX, x_rdy)
            rs_d = op("vector", lambda e: e.reduce_sum(stats[:, 0:1], xbt[:, :], axis=AX.X), sV)
            wait("vector", sV, rs_d)
            op("vector", lambda e: e.tensor_scalar_mul(stats[:, 2:3], stats[:, 0:1], -1.0 / H))
            m_d = op("vector", lambda e: e.tensor_scalar_mul(stats[:, 4:5], stats[:, 0:1], 1.0 / H), sV)
            wait("vector", sV, m_d)
            op("vector", lambda e: e.tensor_tensor(stats[:, 5:6], stats[:, 4:5],
                                                   stats[:, 4:5], AluOpType.mult))
            wait("scalar", sX, x_rdy)
            wait("scalar", sT, xnt_free_t)
            sq_done = op("scalar", lambda e: e.activation(xnt[:, :], xbt[:, :], AF.Square,
                                                          accum_out=stats[:, 1:2]), sA)
            xls_free_a = sq_done
            wait("vector", sA, sq_done)
            t_d = op("vector", lambda e: e.tensor_scalar_mul(stats[:, 3:4], stats[:, 1:2], 1.0 / H), sV)
            wait("vector", sV, t_d)
            var_done = op("vector", lambda e: e.tensor_sub(stats[:, 3:4], stats[:, 3:4],
                                                           stats[:, 5:6]), sV)
            wait("scalar", sV, var_done)
            std_done = op("scalar", lambda e: e.activation(stats[:, 6:7], stats[:, 3:4],
                                                           AF.Sqrt, bias=cst[:, 0:1]), sA)
            wait("vector", sA, std_done)
            r_d = op("vector", lambda e: e.reciprocal(stats[:, 7:8], stats[:, 6:7]), sV)
            wait("vector", sV, r_d)
            xn_done = op("vector", lambda e: e.tensor_scalar(xnt[:, :], xbt[:, :],
                                                             stats[:, 2:3], stats[:, 7:8],
                                                             AluOpType.add, AluOpType.mult), sV)
            xls_free_v = xn_done

            wait("tensor", sV, xn_done)
            if i == 0:
                wait("tensor", sC, idb_rdy)
            last_t = 0
            for fc in range(FC):
                par = fc % 2
                wait("tensor", sV, tp_evac[par])
                last_t = op("tensor", lambda e, fc=fc, par=par: e.matmul(
                    ps[6 + par][:, 0:128], xnt[:, fc * 128:(fc + 1) * 128], idb[:, :],
                    start=True, stop=True), sT)
                wait("vector", sT, last_t)
                if i < 8:
                    ev = op("vector", lambda e, fc=fc, par=par, lr=lr: e.tensor_copy(
                        xnT[:, fc, lr:lr + 128], ps[6 + par][:, 0:128]), sV)
                else:
                    if fc % 8 == 0:
                        wait("vector", sSP, spill_done[0])
                    ev = op("vector", lambda e, fc=fc, par=par: e.tensor_copy(
                        stage[:, 0, (fc % 8) * 128:(fc % 8 + 1) * 128],
                        ps[6 + par][:, 0:128]), sV)
                    if fc % 8 == 7:
                        wait("sync", sV, ev)
                        spill_done[0] = dma("sync", xnpt[:, fc - 7:fc + 1, lr:lr + 128],
                                            stage[:, 0, :], sSP)
                tp_evac[par] = ev
            xnt_free_t = last_t
            if i == 7:
                xnT_done_v = sV.n

        ln_tail_v = sV.n

        if not do_gate:
            n_groups = 0
        wait("tensor", sC, c_all)
        if not do_ln:
            n_groups = 0
        wait("tensor", sV, ln_tail_v)
        mkt_prev_t = 0
        mkt_evac = 0
        lg_evac = {6: 0, 7: 0}
        for tc in range(8 if do_gate else 0):
            gb = 6 + (tc % 2)
            wait("tensor", sV, lg_evac[gb])
            op("tensor", lambda e, tc=tc, gb=gb: e.matmul(
                ps[gb][:, tc * 4:tc * 4 + 4], onesr[:, :], ebr[:, :],
                start=True, stop=False, skip_group_check=True))
            lg_t = None
            for fc in range(FC):
                lg_t = op("tensor", lambda e, tc=tc, fc=fc, gb=gb: e.matmul(
                    ps[gb][:, tc * 4:tc * 4 + 4], xnT[:, fc, tc * 128:(tc + 1) * 128],
                    gws[:, fc, :], start=False, stop=(fc == FC - 1),
                    skip_group_check=True), sT if fc == FC - 1 else None)
            wait("vector", sT, lg_t)
            if tc > 0:
                wait("vector", sT, mkt_prev_t)
            lg_evac[gb] = op("vector", lambda e, tc=tc, gb=gb: e.tensor_copy(
                lg[:, :], ps[gb][:, tc * 4:tc * 4 + 4]), sV)
            wait("vector", sV, lg_evac[gb])
            rm_d = op("vector", lambda e: e.reduce_max(stats[:, 7:8], lg[:, :], axis=AX.X), sV)
            wait("vector", sV, rm_d)
            e_d = op("vector", lambda e: e.tensor_scalar(mkb[:, 0:1], lg[:, 0:1], stats[:, 7:8],
                                                         None, AluOpType.is_equal), sV)
            wait("vector", sV, e_d)
            op("vector", lambda e: e.tensor_copy(stats[:, 4:5], mkb[:, 0:1]))
            mk_done = None
            for k in (1, 2, 3):
                e_d = op("vector", lambda e, k=k: e.tensor_scalar(stats[:, 5:6], lg[:, k:k + 1],
                                                                  stats[:, 7:8], None,
                                                                  AluOpType.is_equal), sV)
                wait("vector", sV, e_d)
                e_d = op("vector", lambda e: e.tensor_tensor(stats[:, 6:7], stats[:, 5:6],
                                                             stats[:, 4:5], AluOpType.mult), sV)
                wait("vector", sV, e_d)
                mk_done = op("vector", lambda e, k=k: e.tensor_tensor(
                    mkb[:, k:k + 1], stats[:, 5:6], stats[:, 6:7], AluOpType.subtract), sV)
                if k < 3:
                    wait("vector", sV, mk_done)
                    op("vector", lambda e, k=k: e.tensor_tensor(stats[:, 4:5], stats[:, 4:5],
                                                                mkb[:, k:k + 1], AluOpType.add))
            if do_gate is True or do_gate >= 2:
                wait("tensor", sV, mk_done)
                wait("tensor", sV, mkt_evac)
                mkt_prev_t = op("tensor", lambda e, tc=tc: e.matmul(
                    ps[5][0:4, 0:128], mkb[:, 0:4], idb[:, :], start=True, stop=True), sT)
                wait("vector", sT, mkt_prev_t)
                mkt_evac = op("vector", lambda e, tc=tc: e.tensor_copy(
                    mkT_a[:, tc * 128:(tc + 1) * 128], ps[5][0:4, 0:128]), sV)
        mkT_done_v = sV.n

        zm_done_v = 0
        mex_evac = 0
        for t2 in range(2 if (do_gate is True or do_gate >= 3) else 0):
            wait("tensor", sV, mkT_done_v)
            wait("tensor", sV, mex_evac)
            wait("tensor", sV, zm_done_v)
            me_t = op("tensor", lambda e, t2=t2: e.matmul(
                ps[4][0:32, :], emat[:, :], mkT_a[:, t2 * 512:(t2 + 1) * 512],
                start=True, stop=True), sT)
            wait("vector", sT, me_t)
            mex_d = op("vector", lambda e, t2=t2: e.tensor_copy(
                mex_a[:, t2 * 512:(t2 + 1) * 512], ps[4][0:32, :]), sV)
            mex_evac = mex_d
            z_t = None
            for fc in range(FC):
                z_t = op("tensor", lambda e, t2=t2, fc=fc: e.matmul(
                    ps[7][0:32, :], asb_ap(fc), xnT[:, fc, t2 * 512:(t2 + 1) * 512],
                    start=(fc == 0), stop=(fc == FC - 1)),
                    sT if fc == FC - 1 else None)
            wait("vector", sT, z_t)
            wait("vector", sV, mex_d)
            zm_done_v = op("vector", lambda e, t2=t2: e.tensor_tensor(
                zmT_a[:, t2 * 512:(t2 + 1) * 512], ps[7][0:32, :],
                mex_a[:, t2 * 512:(t2 + 1) * 512], AluOpType.mult), sV)

        gate_tail_v = sV.n

        ev_bank = {b: gate_tail_v for b in range(8)}

        def pass_prologue(n_banks):
            for b in range(n_banks):
                wait("tensor", sV, ev_bank[b])
            wait("tensor", sA, sA.n)

        def make_gate():
            pre = dict(ev_bank)
            seen = set()
            def gate(b):
                if b not in seen:
                    seen.add(b)
                    wait("tensor", sV, pre[b])
            return gate

        def emit_pass(n_banks, mm_fn, evac_fn, wslab_aps, wwidth, use_par):
            flush_pending()
            pslot = None
            slot = wready = None
            for fc in range(FC):
                if fc % 2 == 0:
                    slot, wready = w_push(wslab_aps[fc // 2], wwidth)
                    if fc > 0:
                        wcons[widx[0] - 2] = sT.n
                if use_par:
                    if fc > 0:
                        pcons[pidx[0] - 1] = sT.n
                    pslot, pready = p_push(fc)
                if fc == 0:
                    wait("tensor", sA, sA.n)
                    gate = make_gate()
                if fc % 2 == 0:
                    wait("tensor", sW, wready)
                if use_par:
                    wait("tensor", sP, pready)
                mm_fn(fc, slot, pslot, gate if fc == 0 else None)
            w_consumed()
            if use_par:
                p_consumed()
            outs_ = []
            pass_end_t = sT.n
            for b in range(n_banks):
                wait("vector", sT, pass_end_t)
                outs_.append(evac_fn(b))
            for b in range(n_banks):
                ev_bank[b] = outs_[b]
            return outs_

        u_state = {}
        pending = [None]
        last_exp = [0]
        prev_pv15 = [0]
        prev_pv14 = [0]
        prev_acc14 = [0]
        prev_acc15 = [0]
        sum_read_t = [0]
        accb_sum_t = [0]

        def finalize_sums(u):
            st = u_state[u]
            wait("tensor", sV, st["accb"])
            wait("tensor", sV, ev_bank[4])
            sum_t = op("tensor", lambda e: e.matmul(
                ps[4][:, :], aones[:, :], accb_a, start=True, stop=True), sT)
            accb_sum_t[0] = sum_t
            wait("vector", sT, sum_t)
            r_d = op("vector", lambda e: e.reciprocal(bcs[:, :], ps[4][:, :]), sV)
            ev_bank[4] = r_d
            st["rsum"] = r_d

        def finalize_ctx(u):
            st = u_state[u]
            wait("vector", sV, st["rsum"])
            wait("vector", sT, st["pv"])
            ctx_d = op("vector", lambda e, h=st["h"], qc=st["qc"], pp=st["pp"]:
                       e.tensor_tensor(ctxT[:, h, qc * 512:(qc + 1) * 512],
                                       ps[2 + pp][:, :], bcs[:, :], AluOpType.mult), sV)
            ev_bank[2 + st["pp"]] = ctx_d
            st["ctx"] = ctx_d
            ev_bank[4] = max(ev_bank[4], ctx_d)

        def flush_pending():
            if pending[0] is not None:
                u = pending[0]
                if "rsum" not in u_state[u]:
                    finalize_sums(u)
                if "ctx" not in u_state[u]:
                    finalize_ctx(u)
                pending[0] = None

        u_cnt = [0]
        vbr_cons = [0]
        for g in range(n_groups):
            wait("gpsimd", sT, vbr_cons[0])
            dma("gpsimd", vbr[:, :], vb_h[0:1, g * 512:(g + 1) * 512], sC)
            vb_rdy = sC.n
            flush_pending()
            wait("tensor", sC, vb_rdy)
            wait("tensor", sV, ev_bank[0])
            wait("tensor", sA, sA.n)
            vb_mm = op("tensor", lambda e: e.matmul(
                ps[0][:, :], onesr[:, :], vbr[:, :], start=True, stop=True), sT)
            vbr_cons[0] = vb_mm
            wait("vector", sT, vb_mm)
            vb_cp = op("vector", lambda e: e.tensor_copy(vbbc_a, ps[0][:, :]), sV)
            ev_bank[0] = vb_cp

            for hp in range(2):
                kc0 = hp * 8
                vslabs = [wqkv[:, 2 * j:2 * j + 2, 2 * H + g * 512: 2 * H + (g + 1) * 512]
                          for j in range(FC // 2)]
                use_par = (hp == 1)

                def v_mm(fc, slot, pslot, gate=None, kc0=kc0, use_par=use_par):
                    for tk in range(8):
                        if gate:
                            gate(tk)
                        if use_par:
                            lhs = xpr[:, pslot, tk * 128:(tk + 1) * 128]
                        else:
                            lhs = xnT[:, fc, tk * 128:(tk + 1) * 128]
                        op("tensor", lambda e, lhs=lhs, tk=tk, fc=fc, slot=slot: e.matmul(
                            ps[tk][:, :], lhs, wring[:, slot, fc % 2, 0:512],
                            start=(fc == 0), stop=(fc == FC - 1)),
                           sT if tk == 7 else None)

                def v_evac(b, kc0=kc0, vb_cp=vb_cp):
                    wait("vector", sV, vb_cp)
                    return op("vector", lambda e, b=b, kc0=kc0: e.tensor_tensor(
                        Vg[:, kc0 + b, :], ps[b][:, :], vbbc_a, AluOpType.add), sV)

                emit_pass(8, v_mm, v_evac, vslabs, 512, use_par)

            for sg in range(2):
                ha = 4 * g + 2 * sg
                kslabs = [wqkv[:, 2 * j:2 * j + 2, H + ha * 128: H + (ha + 2) * 128]
                          for j in range(FC // 2)]

                def k_mm(fc, slot, pslot, gate=None, ha=ha):
                    for hl in range(2):
                        for tc in range(4):
                            if gate:
                                gate(hl * 4 + tc)
                            if tc < 2:
                                rhs = xnT[:, fc, tc * 512:(tc + 1) * 512]
                            else:
                                rhs = xpr[:, pslot, (tc - 2) * 512:(tc - 1) * 512]
                            op("tensor", lambda e, hl=hl, tc=tc, rhs=rhs, slot=slot, fc=fc:
                               e.matmul(ps[hl * 4 + tc][:, :],
                                        wring[:, slot, fc % 2, hl * 128:(hl + 1) * 128],
                                        rhs, start=(fc == 0), stop=(fc == FC - 1)),
                               sT if (hl == 1 and tc == 3) else None)

                def k_evac(b, ha=ha):
                    hl, tc = b // 4, b % 4
                    return op("vector", lambda e, hl=hl, tc=tc: e.tensor_scalar(
                        kT[:, hl, tc * 512:(tc + 1) * 512], ps[hl * 4 + tc][:, :],
                        qkvb[:, 32 + ha + hl:33 + ha + hl], None, AluOpType.add), sV)

                emit_pass(8, k_mm, k_evac, kslabs, 256, True)

                qslabs = [wqkv[:, 2 * j:2 * j + 2, ha * 128:(ha + 2) * 128]
                          for j in range(FC // 2)]

                def q_mm(fc, slot, pslot, gate=None, ha=ha):
                    for hl in range(2):
                        for qc in range(2):
                            if gate:
                                gate(hl * 2 + qc)
                            op("tensor", lambda e, hl=hl, qc=qc, slot=slot, fc=fc:
                               e.matmul(ps[hl * 2 + qc][:, :],
                                        wring[:, slot, fc % 2, hl * 128:(hl + 1) * 128],
                                        xnT[:, fc, qc * 512:(qc + 1) * 512],
                                        start=(fc == 0), stop=(fc == FC - 1)),
                               sT if (hl == 1 and qc == 1) else None)

                def q_evac(b, ha=ha):
                    hl, qc = b // 2, b % 2
                    op("vector", lambda e, hl=hl, qc=qc: e.tensor_scalar(
                        outs[:, 0, :], ps[hl * 2 + qc][:, :],
                        qkvb[:, ha + hl:ha + hl + 1], SC1, AluOpType.add,
                        AluOpType.mult), sV)
                    t2 = op("vector", lambda e, hl=hl, qc=qc: e.tensor_scalar(
                        outs[:, 1, :], ps[hl * 2 + qc][:, :],
                        qkvb[:, ha + hl:ha + hl + 1], SC2, AluOpType.add,
                        AluOpType.mult), sV)
                    wait("vector", sV, t2)
                    return op("vector", lambda e, hl=hl, qc=qc: e.tensor_tensor(
                        qT[:, hl, qc * 512:(qc + 1) * 512], outs[:, 0, :],
                        outs[:, 1, :], AluOpType.add), sV)

                emit_pass(4, q_mm, q_evac, qslabs, 256, False)
                sg_ready = sV.n
                wait("tensor", sV, sg_ready)

                for hl in range(2):
                    h = ha + hl
                    for qc in range(2):
                        u = u_cnt[0]
                        u_cnt[0] += 1
                        pp = u % 2
                        hloc = 2 * sg + hl
                        SB = (0, 1, 5)
                        mm_s = {}
                        exp_done = {}
                        pv_mm = {}
                        acc_add = {}
                        if u >= 2 and (u - 2) in u_state:
                            wait("tensor", sV, u_state[u - 2].get("ctx", 0))
                        for step in range(18):
                            if step < 16:
                                kc = step
                                spar = SB[kc % 3]
                                if kc >= 3:
                                    wait("tensor", sA, exp_done[kc - 3])
                                elif kc == 0:
                                    wait("tensor", sA, last_exp[0])
                                mm_s[kc] = op("tensor", lambda e, kc=kc, spar=spar,
                                              hl=hl, qc=qc: e.matmul(
                                                  ps[spar][:, :],
                                                  kT[:, hl, kc * 128:(kc + 1) * 128],
                                                  qT[:, hl, qc * 512:(qc + 1) * 512],
                                                  start=True, stop=True), sT)
                            if 1 <= step <= 16:
                                k = step - 1
                                wait("scalar", sT, mm_s[k])
                                if k >= 4:
                                    wait("scalar", sT, pv_mm[k - 4])
                                    wait("scalar", sV, acc_add[k - 4])
                                elif k < 2:
                                    wait("scalar", sT, prev_pv14[0])
                                    wait("scalar", sV, prev_acc14[0])
                                else:
                                    wait("scalar", sT, prev_pv15[0])
                                    wait("scalar", sV, prev_acc15[0])
                                exp_done[k] = op("scalar", lambda e, k=k, sb=SB[k % 3]:
                                                 e.activation(expt(k % 4), ps[sb][:, :],
                                                              AF.Exp), sA)
                                wait("vector", sA, exp_done[k])
                                if k < 2:
                                    acc_add[k] = op("vector", lambda e, k=k: e.tensor_copy(
                                        acc[:, k % 2, :], expt(k % 4)), sV)
                                else:
                                    wait("vector", sV, acc_add[k - 2])
                                    acc_add[k] = op("vector", lambda e, k=k: e.tensor_tensor(
                                        acc[:, k % 2, :], acc[:, k % 2, :], expt(k % 4),
                                        AluOpType.add), sV)
                            if 2 <= step <= 17:
                                k = step - 2
                                wait("tensor", sA, exp_done[k])
                                pv_mm[k] = op("tensor", lambda e, k=k, pp=pp, hloc=hloc:
                                              e.matmul(ps[2 + pp][:, :],
                                                       Vg[:, k, hloc * 128:(hloc + 1) * 128],
                                                       expt(k % 4),
                                                       start=(k == 0), stop=(k == 15)), sT)
                            if step == 2 and pending[0] is not None:
                                finalize_sums(pending[0])
                            if step == 6 and pending[0] is not None:
                                finalize_ctx(pending[0])
                                pending[0] = None
                        last_exp[0] = exp_done[15]
                        prev_pv14[0] = pv_mm[13]
                        prev_pv15[0] = pv_mm[15]
                        prev_acc14[0] = acc_add[13]
                        prev_acc15[0] = acc_add[15]
                        wait("vector", sT, accb_sum_t[0])
                        wait("vector", sV, acc_add[15])
                        accb_d = op("vector", lambda e: e.tensor_tensor(
                            accb_a, acc[:, 0, :], acc[:, 1, :], AluOpType.add), sV)
                        u_state[u] = dict(pv=pv_mm[15], accb=accb_d, h=h, qc=qc, pp=pp)
                        pending[0] = u

        flush_pending()
        attn_tail_v = sV.n
        if u_cnt[0] == 0:
            attn_tail_v = sV.n

        resid_rdy = {}
        resid_free = {0: attn_tail_v, 1: attn_tail_v}
        outs_prev = {0: 0, 1: 0, 2: 0, 3: 0}
        oslot_ap = (outs[:, 0, :], outs[:, 1, :], acc[:, 0, :], acc[:, 1, :])
        misc_cons = {0: 0, 1: 0}
        misc_rdy_at = {}
        for b in range(8):
            ev_bank[b] = max(ev_bank[b], attn_tail_v)
        if do_out:
            dma("gpsimd", obr[:, 0, :], ob_h[0:1, 0:512], sC)
            dma("gpsimd", bbs[:, 0, :], bb_h[:, 0:512], sC)
            misc_rdy_at[0] = sC.n
        for oc in range(8 if do_out else 0):
            if oc + 1 < 8:
                nsl = (oc + 1) % 2
                wait("gpsimd", sT, misc_cons[nsl])
                dma("gpsimd", obr[:, nsl, :],
                    ob_h[0:1, (oc + 1) * 512:(oc + 2) * 512], sC)
                dma("gpsimd", bbs[:, nsl, :],
                    bb_h[:, (oc + 1) * 512:(oc + 2) * 512], sC)
                misc_rdy_at[oc + 1] = sC.n
            misc_rdy = misc_rdy_at[oc]
            oslabs = ([wo[:, 2 * j:2 * j + 2, oc * 512:(oc + 1) * 512]
                       for j in range(FC // 2)] +
                      [wup[:, 2 * j:2 * j + 2, oc * 512:(oc + 1) * 512]
                       for j in range(FC // 2)])
            last_mm = {}
            slot = wready = None
            for f in range(64):
                if f % 2 == 0:
                    slot, wready = w_push(oslabs[f // 2], 512)
                    if f > 0:
                        wcons[widx[0] - 2] = sT.n
                if f == 0:
                    wait("tensor", sA, sA.n)
                    gate = make_gate()
                    wait("tensor", sC, misc_rdy)
                    for tc in range(8):
                        gate(tc)
                        op("tensor", lambda e, tc=tc, oc=oc: e.matmul(
                            ps[tc][:, :], onesr[:, :], obr[:, oc % 2, :],
                            start=True, stop=False, skip_group_check=True))
                        op("tensor", lambda e, tc=tc, oc=oc: e.matmul(
                            ps[tc][:, :], zmT_a[:, tc * 128:(tc + 1) * 128],
                            bbs[:, oc % 2, :],
                            start=False, stop=False, skip_group_check=True))
                if f % 2 == 0:
                    wait("tensor", sW, wready)
                src = ctxT if f < 32 else xnT
                fc = f % 32
                for tc in range(8):
                    mm = op("tensor", lambda e, src=src, fc=fc, tc=tc, slot=slot, f=f:
                            e.matmul(ps[tc][:, :], src[:, fc, tc * 128:(tc + 1) * 128],
                                     wring[:, slot, f % 2, 0:512], start=False,
                                     stop=(f == 63), skip_group_check=True),
                            sT if (tc == 7 or f == 63) else None)
                    if f == 63:
                        last_mm[tc] = mm
            w_consumed()
            misc_cons[oc % 2] = last_mm[7]
            for tc in range(8):
                rsl = tc % 2
                osl = tc % 4
                wait("sync", sV, resid_free[rsl])
                resid_rdy[tc] = dma("sync", xrr[:, rsl, :],
                                    x_own[tc * 128:(tc + 1) * 128,
                                          oc * 512:(oc + 1) * 512], sR)
                wait("vector", sT, last_mm[tc])
                wait("vector", sR, resid_rdy[tc])
                wait("vector", sO, outs_prev[osl])
                ev = op("vector", lambda e, tc=tc, osl=osl, rsl=rsl: e.tensor_tensor(
                    oslot_ap[osl], ps[tc][:, :], xrr[:, rsl, :],
                    AluOpType.add), sV)
                ev_bank[tc] = ev
                resid_free[rsl] = ev
                wait("sync", sV, ev)
                outs_prev[osl] = dma("sync", out_d[tc * 128:(tc + 1) * 128,
                                                   oc * 512:(oc + 1) * 512],
                                     oslot_ap[osl], sO)
        if dbg:
            sD = Sem(ctx, nc, "sD")
            wait("gpsimd", sV, sV.n)
            wait("gpsimd", sT, sT.n)
            wait("gpsimd", sA, sA.n)
            dma("gpsimd", d_xnT[:, :, :], xnT[:, :, :], sD)
            dma("gpsimd", d_kT[:, :, :], kT[:, :, :], sD)
            dma("gpsimd", d_qT[:, :, :], qT[:, :, :], sD)
            dma("gpsimd", d_Vg[:, :, :], Vg[:, :, :], sD)
            dma("gpsimd", d_ctxT[:, :, :], ctxT[:, :, :], sD)
            dma("gpsimd", d_zm[:, :], zmT_a, sD)
            dma("gpsimd", d_xnp[:, :, :], xnpt[:, :, :], sD)
            dma("gpsimd", d_xbt[:, :], xbt[:, :], sD)
            dma("gpsimd", d_xnt[:, :], xnt[:, :], sD)
            dma("sync", d_st[:, :], stats[:, :], sD) if False else None
            dma("gpsimd", d_st[:, :], stats[:, :], sD)
            dma("gpsimd", d_cst[:, :], cst[:, :], sD)
            wait("gpsimd", sD, sD.n)
        wait("sync", sO, sO.n)

        block = ctx.enter_context(nc.Block())

        def runner(fns):
            def f(eng):
                for fn in fns:
                    fn(eng)
            return f

        block.sync(runner(plan["sync"]))
        block.gpsimd(runner(plan["gpsimd"]))
        block.scalar(runner(plan["scalar"]))
        block.vector(runner(plan["vector"]))
        block.tensor(runner(plan["tensor"]))
    return nc


def kernel(L0_attn_x, L0_attn_attn_qkvw, L0_attn_attn_qkvb, L0_attn_attn_ow,
           L0_attn_attn_ob, L0_ffn_W_up, L0_ffn_gate_w, L0_ffn_expert_bias,
           L0_ffn_lora_A, L0_ffn_lora_B):
    x = np.ascontiguousarray(np.asarray(L0_attn_x, dtype=np.float32))
    wqkv = np.ascontiguousarray(np.asarray(L0_attn_attn_qkvw, dtype=np.float32)
                                .reshape(FC, 128, 3 * H).transpose(1, 0, 2))
    wo = np.ascontiguousarray(np.asarray(L0_attn_attn_ow, dtype=np.float32)
                              .reshape(FC, 128, H).transpose(1, 0, 2))
    wup = np.ascontiguousarray(np.ascontiguousarray(
        np.asarray(L0_ffn_W_up, dtype=np.float32).T)
        .reshape(FC, 128, H).transpose(1, 0, 2))
    qkvb = np.asarray(L0_attn_attn_qkvb, dtype=np.float32)
    qkvb_t = np.ascontiguousarray(qkvb.reshape(96, 128).T)
    vb_row = np.ascontiguousarray(qkvb[2 * H:].reshape(1, H))
    ob_row = np.ascontiguousarray(np.asarray(L0_attn_attn_ob, dtype=np.float32
                                             ).reshape(1, H))
    gw3 = np.ascontiguousarray(np.asarray(L0_ffn_gate_w, dtype=np.float32
                                          ).T.reshape(FC, 128, 4))
    eb_row = np.ascontiguousarray(np.asarray(L0_ffn_expert_bias, dtype=np.float32
                                             ).reshape(1, 4))
    a3 = np.ascontiguousarray(np.asarray(L0_ffn_lora_A, dtype=np.float32
                                         ).transpose(2, 0, 1).reshape(FC, 128, 32))
    bb = np.ascontiguousarray(np.asarray(L0_ffn_lora_B, dtype=np.float32
                                         ).transpose(0, 2, 1).reshape(32, H))
    ident = np.eye(128, dtype=np.float32)
    ones_r = np.ones((1, 128), dtype=np.float32)
    ones_c = np.ones((128, 1), dtype=np.float32)
    allones = np.ones((128, 128), dtype=np.float32)
    emat = np.kron(np.eye(4, dtype=np.float32), np.ones((1, 8), dtype=np.float32))

    nc = build()
    in_maps = []
    for c in range(8):
        b, half = c // 2, c % 2
        xo = np.ascontiguousarray(x[b, half * TOK:(half + 1) * TOK])
        xp = np.ascontiguousarray(x[b, (1 - half) * TOK:(2 - half) * TOK])
        in_maps.append(dict(x_own=xo, x_par=xp, wqkv=wqkv, wo=wo, wup=wup,
                            qkvb_t=qkvb_t, vb_row=vb_row, ob_row=ob_row,
                            gw3=gw3, eb_row=eb_row, a3=a3, bb=bb, ident=ident,
                            ones_r=ones_r, ones_c=ones_c, emat=emat,
                            allones=allones))
    res = run_bass_kernel_spmd(nc, in_maps, core_ids=list(range(8)))
    out = np.empty((4, 2048, H), dtype=np.asarray(res.results[0]["out"]).dtype)
    for c in range(8):
        b, half = c // 2, c % 2
        out[b, half * TOK:(half + 1) * TOK] = res.results[c]["out"]
    return out
